# version 02
# baseline (speedup 1.0000x reference)
from contextlib import ExitStack

import numpy as np
import ml_dtypes
import concourse.bass as bass
import concourse.mybir as mybir
from concourse.bass_utils import run_bass_kernel_spmd
from concourse.alu_op_type import AluOpType

F32 = mybir.dt.float32
BF16 = mybir.dt.bfloat16
AF = mybir.ActivationFunctionType
NP_BF16 = ml_dtypes.bfloat16

B, T, D = 4, 4096, 1024
NCORE = 8
TC = T // 2
NH, HD = 16, 64
NEG = -30000.0


def host_rope_R():
    theta = 1.0 / 10000.0 ** (np.arange(0, 32, 2, dtype=np.float64) / 32.0)
    out = np.zeros((8, 128, 128), np.float32)
    for h in range(NH):
        c, off = divmod(h * 64, 128)
        cs = np.cos(h * theta).astype(np.float32)
        sn = np.sin(h * theta).astype(np.float32)
        blk = np.zeros((64, 64), np.float32)
        for half in (0, 32):
            for j in range(16):
                e, o = half + 2 * j, half + 2 * j + 1
                blk[e, half + j] = cs[j]
                blk[o, half + j] = -sn[j]
                blk[e, half + 16 + j] = sn[j]
                blk[o, half + 16 + j] = cs[j]
        out[c, off:off + 64, off:off + 64] = blk
    return out.astype(NP_BF16)



class Cnt:

    def __init__(self):
        self.v = {}

    def inc(self, name, amt=1):
        self.v[name] = self.v.get(name, 0) + amt
        return self.v[name]

    def get(self, name, default=0):
        return self.v.get(name, default)



def build_l0_qkv():
    nc = bass.Bass("TRN2", target_bir_lowering=False)
    xT = nc.dram_tensor("xT", [D, TC], F32, kind="ExternalInput").ap()
    wq = nc.dram_tensor("wq", [D, D], F32, kind="ExternalInput").ap()
    wk = nc.dram_tensor("wk", [D, HD], F32, kind="ExternalInput").ap()
    wv = nc.dram_tensor("wv", [D, HD], F32, kind="ExternalInput").ap()
    bq = nc.dram_tensor("bq", [D], F32, kind="ExternalInput").ap()
    bk = nc.dram_tensor("bk", [HD], F32, kind="ExternalInput").ap()
    bv = nc.dram_tensor("bv", [HD], F32, kind="ExternalInput").ap()
    ropeR = nc.dram_tensor("ropeR", [8, 128, 128], BF16, kind="ExternalInput").ap()
    q_out = nc.dram_tensor("q", [D, TC], BF16, kind="ExternalOutput").ap()
    k_out = nc.dram_tensor("k", [HD, TC], BF16, kind="ExternalOutput").ap()
    v_out = nc.dram_tensor("v", [TC, HD], BF16, kind="ExternalOutput").ap()

    NTC = TC // 512
    c = Cnt()
    with ExitStack() as _st:
        xb = _st.enter_context(nc.sbuf_tensor("xb", [128, 8 * TC], BF16))
        wqb = _st.enter_context(nc.sbuf_tensor("wqb", [128, 8 * D], BF16))
        wkf = _st.enter_context(nc.sbuf_tensor("wkf", [128, 8 * HD], F32))
        wkp = _st.enter_context(nc.sbuf_tensor("wkp", [128, 8 * HD], BF16))
        wvb = _st.enter_context(nc.sbuf_tensor("wvb", [128, 8 * HD], BF16))
        Rb = _st.enter_context(nc.sbuf_tensor("Rb", [128, 8 * 128], BF16))
        bqc = _st.enter_context(nc.sbuf_tensor("bqc", [128, 8], F32))
        bkc = _st.enter_context(nc.sbuf_tensor("bkc", [64, 1], F32))
        bvr = _st.enter_context(nc.sbuf_tensor("bvr", [1, HD], BF16))
        ones1 = _st.enter_context(nc.sbuf_tensor("ones1", [1, 128], BF16))
        qp = _st.enter_context(nc.sbuf_tensor("qp", [128, 8 * TC], BF16))
        qr = _st.enter_context(nc.sbuf_tensor("qr", [128, 8 * TC], BF16))
        ks = _st.enter_context(nc.sbuf_tensor("ks", [64, TC], BF16))
        vs = _st.enter_context(nc.sbuf_tensor("vs", [128, 16 * HD], BF16))
        psA = _st.enter_context(nc.psum_tensor("psA", [128, 512], F32))
        psB = _st.enter_context(nc.psum_tensor("psB", [128, 512], F32))
        psR = _st.enter_context(nc.psum_tensor("psR", [128, 512], F32))
        psK = _st.enter_context(nc.psum_tensor("psK", [128, 512], F32))
        psV1 = _st.enter_context(nc.psum_tensor("psV1", [128, 512], F32))
        psV2 = _st.enter_context(nc.psum_tensor("psV2", [128, 512], F32))
        s_dh = _st.enter_context(nc.semaphore("s_dh"))
        s_ds = _st.enter_context(nc.semaphore("s_ds"))
        s_t = _st.enter_context(nc.semaphore("s_t"))
        s_s = _st.enter_context(nc.semaphore("s_s"))
        s_v = _st.enter_context(nc.semaphore("s_v"))
        s_st = _st.enter_context(nc.semaphore("s_st"))
        block = _st.enter_context(nc.Block())
        def sync_loads(sync):
            for cc in range(8):
                sync.dma_start(out=wkf[:, cc * HD:(cc + 1) * HD],
                               in_=wk[cc * 128:(cc + 1) * 128, :]).then_inc(s_dh, 16)
                c.inc("dh", 16)
            for cc in range(8):
                sync.dma_start(out=bqc[:, cc:cc + 1],
                               in_=bq[cc * 128:(cc + 1) * 128].rearrange("(p o) -> p o", o=1)
                               ).then_inc(s_dh, 16)
                c.inc("dh", 16)
            for i, base in enumerate((0, 1, 32, 33)):
                sync.dma_start(out=bkc[16 * i:16 * (i + 1), :],
                               in_=bk[base:min(base + 32, HD):2].rearrange("(p o) -> p o", o=1),
                               allow_slow_non_contiguous=True).then_inc(s_dh, 16)
                c.inc("dh", 16)
            for cc in range(8):
                sync.dma_start(out=Rb[:, cc * 128:(cc + 1) * 128],
                               in_=ropeR[cc, :, :]).then_inc(s_dh, 16)
                c.inc("dh", 16)

        def gp_loads(gpsimd):
            for cc in range(8):
                gpsimd.dma_start(out=xb[:, cc * TC:(cc + 1) * TC],
                                 in_=xT[cc * 128:(cc + 1) * 128, :]).then_inc(s_ds, 16)
                gpsimd.dma_start(out=wqb[:, cc * D:(cc + 1) * D],
                                 in_=wq[cc * 128:(cc + 1) * 128, :]).then_inc(s_ds, 16)
                gpsimd.dma_start(out=wvb[:, cc * HD:(cc + 1) * HD],
                                 in_=wv[cc * 128:(cc + 1) * 128, :]).then_inc(s_ds, 16)
                c.inc("ds", 48)
            gpsimd.dma_start(out=bvr[:, :], in_=bv.rearrange("(o h) -> o h", o=1)
                             ).then_inc(s_ds, 16)
            c.inc("ds", 16)

        block.sync(sync_loads)
        block.gpsimd(gp_loads)

        def v_perm(vector):
            vector.wait_ge(s_dh, c.get("dh"))
            vector.memset(ones1[:, :], 1.0)
            last = None
            for cc in range(8):
                src = wkf[:, cc * HD:(cc + 1) * HD]
                dst = wkp[:, cc * HD:(cc + 1) * HD]
                for i, base in enumerate((0, 1, 32, 33)):
                    last = vector.tensor_copy(out=dst[:, 16 * i:16 * (i + 1)],
                                              in_=src[:, base:min(base + 32, HD):2])
            last.then_inc(s_v, 1)
            c.inc("v", 1)

        block.vector(v_perm)
        V_PERM = c.get("v")

        def t_mm(tensor):
            tensor.wait_ge(s_ds, c.get("ds"))
            tensor.wait_ge(s_v, V_PERM)
            for oc in range(8):
                for tci in range(NTC):
                    gi = oc * NTC + tci
                    ps = psA if gi % 2 == 0 else psB
                    if gi >= 2:
                        tensor.wait_ge(s_s, gi - 1)
                    mm = None
                    for ic in range(8):
                        mm = tensor.matmul(
                            ps[:, :],
                            wqb[:, ic * D + oc * 128: ic * D + (oc + 1) * 128],
                            xb[:, ic * TC + tci * 512: ic * TC + (tci + 1) * 512],
                            start=(ic == 0), stop=(ic == 7))
                    mm.then_inc(s_t, 1)
                    c.inc("t", 1)
            for oc in range(8):
                for tci in range(NTC):
                    gi = oc * NTC + tci
                    tensor.wait_ge(s_s, gi + 1)
                    if gi >= 1:
                        tensor.wait_ge(s_v, V_PERM + gi)
                    tensor.matmul(psR[:, :], Rb[:, oc * 128:(oc + 1) * 128],
                                  qp[:, oc * TC + tci * 512: oc * TC + (tci + 1) * 512],
                                  start=True, stop=True).then_inc(s_t, 1)
                    c.inc("t", 1)
            for tci in range(NTC):
                if tci >= 1:
                    tensor.wait_ge(s_s, 32 + tci)
                mm = None
                for ic in range(8):
                    mm = tensor.matmul(
                        psK[0:64, :], wkp[:, ic * HD:(ic + 1) * HD],
                        xb[:, ic * TC + tci * 512: ic * TC + (tci + 1) * 512],
                        start=(ic == 0), stop=(ic == 7))
                mm.then_inc(s_t, 1)
                c.inc("t", 1)
            for tt in range(16):
                ps = psV1 if tt % 2 == 0 else psV2
                if tt >= 2:
                    tensor.wait_ge(s_v, V_PERM + 32 + tt - 1)
                for ic in range(8):
                    tensor.matmul(ps[0:128, 0:HD],
                                  xb[:, ic * TC + tt * 128: ic * TC + (tt + 1) * 128],
                                  wvb[:, ic * HD:(ic + 1) * HD],
                                  start=(ic == 0), stop=False)
                tensor.matmul(ps[0:128, 0:HD], ones1[:, :], bvr[:, :],
                              start=False, stop=True).then_inc(s_t, 1)
                c.inc("t", 1)

        block.tensor(t_mm)

        def s_evac(scalar):
            for oc in range(8):
                for tci in range(NTC):
                    gi = oc * NTC + tci
                    ps = psA if gi % 2 == 0 else psB
                    scalar.wait_ge(s_t, gi + 1)
                    scalar.activation(
                        qp[:, oc * TC + tci * 512: oc * TC + (tci + 1) * 512],
                        ps[:, :], AF.Identity, bias=bqc[:, oc:oc + 1], scale=1.0
                    ).then_inc(s_s, 1)
                    c.inc("s", 1)
            for tci in range(NTC):
                scalar.wait_ge(s_t, 64 + tci + 1)
                scalar.activation(ks[:, tci * 512:(tci + 1) * 512], psK[0:64, :],
                                  AF.Identity, bias=bkc[:, :], scale=1.0
                                  ).then_inc(s_s, 1)
                c.inc("s", 1)

        block.scalar(s_evac)

        def v_evac(vector):
            for oc in range(8):
                for tci in range(NTC):
                    gi = oc * NTC + tci
                    vector.wait_ge(s_t, 32 + gi + 1)
                    vector.tensor_copy(
                        out=qr[:, oc * TC + tci * 512: oc * TC + (tci + 1) * 512],
                        in_=psR[:, :]).then_inc(s_v, 1)
                    c.inc("v", 1)
            for tt in range(16):
                ps = psV1 if tt % 2 == 0 else psV2
                vector.wait_ge(s_t, 68 + tt + 1)
                vector.tensor_copy(out=vs[:, tt * HD:(tt + 1) * HD],
                                   in_=ps[0:128, 0:HD]).then_inc(s_v, 1)
                c.inc("v", 1)

        block.vector(v_evac)

        def sync_stores(sync):
            sync.wait_ge(s_v, c.get("v"))
            sync.wait_ge(s_s, c.get("s"))
            nst = 0
            for cc in range(8):
                sync.dma_start(out=q_out[cc * 128:(cc + 1) * 128, :],
                               in_=qr[:, cc * TC:(cc + 1) * TC]).then_inc(s_st, 16)
                nst += 16
            sync.dma_start(out=k_out[:, :], in_=ks[:, :]).then_inc(s_st, 16)
            nst += 16
            for tt in range(16):
                sync.dma_start(out=v_out[tt * 128:(tt + 1) * 128, :],
                               in_=vs[:, tt * HD:(tt + 1) * HD]).then_inc(s_st, 16)
                nst += 16
            sync.wait_ge(s_st, nst)

        block.sync(sync_stores)
    return nc




def _emit(engine, prog, sems):
    for item in prog:
        if item[0] == "wait":
            engine.wait_ge(sems[item[1]], item[2])
        else:
            inst = item[1](engine)
            if item[2] is not None:
                inst.then_inc(sems[item[2][0]], item[2][1])


def build_attn(mqa):
    nc = bass.Bass("TRN2", target_bir_lowering=False)
    QROWS = 512
    qh_t = nc.dram_tensor("qh", [QROWS, T], BF16, kind="ExternalInput").ap()
    kf_t = nc.dram_tensor("kf", [HD if mqa else QROWS, T], BF16, kind="ExternalInput").ap()
    nvt = 32 if mqa else 8 * 32
    va_t = nc.dram_tensor("vaug", [128, nvt * 65], BF16, kind="ExternalInput").ap()
    tri_t = nc.dram_tensor("tri", [128, 512], BF16, kind="ExternalInput").ap()
    i128_t = nc.dram_tensor("i128", [128, 128], BF16, kind="ExternalInput").ap()
    hm_t = nc.dram_tensor("hm", [5, 128, 8], BF16, kind="ExternalInput").ap()
    i8_t = nc.dram_tensor("i8", [8, 8], F32, kind="ExternalInput").ap()
    ones8_t = nc.dram_tensor("ones8", [8, 128], F32, kind="ExternalInput").ap()
    onesf_t = nc.dram_tensor("onesf", [1, 64], BF16, kind="ExternalInput").ap()
    attn_t = nc.dram_tensor("attn", [QROWS, T], BF16, kind="ExternalOutput").ap()

    NQB = T // 512
    c = Cnt()
    with ExitStack() as _st:
        qh = _st.enter_context(nc.sbuf_tensor("qh_s", [128, 4 * T], BF16))
        kf = _st.enter_context(nc.sbuf_tensor("kf_s", [128, (1 if mqa else 4) * T], BF16))
        va = _st.enter_context(nc.sbuf_tensor("va_s", [128, nvt * 65], BF16))
        tri = _st.enter_context(nc.sbuf_tensor("tri_s", [128, 512], BF16))
        i128 = _st.enter_context(nc.sbuf_tensor("i128_s", [128, 128], BF16))
        hm = _st.enter_context(nc.sbuf_tensor("hm_s", [128, 5 * 8], BF16))
        i8 = _st.enter_context(nc.sbuf_tensor("i8_s", [8, 8], F32))
        ones8 = _st.enter_context(nc.sbuf_tensor("ones8_s", [8, 128], F32))
        onesf = _st.enter_context(nc.sbuf_tensor("onesf_s", [1, 64], BF16))
        n2 = _st.enter_context(nc.sbuf_tensor("n2", [8, 2 * T], BF16))
        mx = _st.enter_context(nc.sbuf_tensor("mx", [8, 4], F32))
        i8n = _st.enter_context(nc.sbuf_tensor("i8n", [8, 8], F32))
        negm = _st.enter_context(nc.sbuf_tensor("negm", [128, 8], F32))
        pbuf = _st.enter_context(nc.sbuf_tensor("pbuf", [128, 7 * 512], BF16))
        drow = _st.enter_context(nc.sbuf_tensor("drow", [1, 4 * 512], F32))
        drowb = _st.enter_context(nc.sbuf_tensor("drowb", [1, 4 * 512], BF16))
        drow2 = _st.enter_context(nc.sbuf_tensor("drow2", [1, 4 * 512], F32))
        bco = _st.enter_context(nc.sbuf_tensor("bco", [128, 512], BF16))
        att = None
        psS = [_st.enter_context(nc.psum_tensor(f"psS{i}", [128, 512], F32)) for i in range(2)]
        psPV = [_st.enter_context(nc.psum_tensor(f"psPV{i}", [128, 512], F32)) for i in range(4)]
        psM = _st.enter_context(nc.psum_tensor("psM", [128, 512], F32))
        psC = _st.enter_context(nc.psum_tensor("psC", [128, 512], F32))
        s_dh = _st.enter_context(nc.semaphore("s_dh"))
        s_t = _st.enter_context(nc.semaphore("s_t"))
        s_s = _st.enter_context(nc.semaphore("s_s"))
        s_v = _st.enter_context(nc.semaphore("s_v"))
        s_st = _st.enter_context(nc.semaphore("s_st"))
        block = _st.enter_context(nc.Block())
        sems = {"dh": s_dh, "t": s_t, "s": s_s, "v": s_v, "st": s_st}

        def k_lhsT(h, kt):
            p0 = (h % 2) * 64
            off = 0 if mqa else (h // 2) * T
            return kf[p0:p0 + 64, off + kt * 128:off + (kt + 1) * 128]

        def va_lhsT(h, kt):
            i = kt if mqa else (h * 32 + kt)
            return va[:, i * 65:(i + 1) * 65]

        def sync_loads(sync):
            for t in range(4):
                sync.dma_start(out=qh[:, t * T:(t + 1) * T],
                               in_=qh_t[t * 128:(t + 1) * 128, :]).then_inc(s_dh, 16)
                c.inc("dh", 16)
            if mqa:
                sync.dma_start(out=kf[0:64, 0:T], in_=kf_t[:, :]).then_inc(s_dh, 16)
                sync.dma_start(out=kf[64:128, 0:T], in_=kf_t[:, :]).then_inc(s_dh, 16)
                c.inc("dh", 32)
            else:
                for t in range(4):
                    sync.dma_start(out=kf[:, t * T:(t + 1) * T],
                                   in_=kf_t[t * 128:(t + 1) * 128, :]).then_inc(s_dh, 16)
                    c.inc("dh", 16)
            for j in range(0, nvt * 65, 2080):
                w = min(2080, nvt * 65 - j)
                sync.dma_start(out=va[:, j:j + w], in_=va_t[:, j:j + w]).then_inc(s_dh, 16)
                c.inc("dh", 16)
            for dst, srct in ((tri, tri_t), (i128, i128_t), (i8, i8_t),
                              (ones8, ones8_t), (onesf, onesf_t)):
                sync.dma_start(out=dst[:, :], in_=srct[:, :]).then_inc(s_dh, 16)
                c.inc("dh", 16)
            for t in range(5):
                sync.dma_start(out=hm[:, t * 8:(t + 1) * 8], in_=hm_t[t, :, :]).then_inc(s_dh, 16)
                c.inc("dh", 16)

        block.sync(sync_loads)
        LOADS_DH = c.get("dh")

        t_prog, s_prog, v_prog = [], [], []

        def wait(prog, sem, val):
            if val > 0:
                prog.append(("wait", sem, val))

        def run(prog, fn, cname):
            if cname in ("s", "v") and c.get(cname) > 0:
                prog.append(("wait", cname, c.get(cname)))
            amt = None
            if cname is not None:
                c.inc(cname, 1)
                amt = (cname, 1)
            prog.append(("op", fn, amt))
            return c.get(cname) if cname else None

        _cm_qsq = nc.sbuf_tensor("qsq", [128, 4 * T], BF16); qsq = _cm_qsq.__enter__()
        wait(v_prog, "dh", LOADS_DH)
        for t in range(4):
            run(v_prog, (lambda t=t: lambda v: v.tensor_tensor(
                out=qsq[:, t * T:(t + 1) * T], in0=qh[:, t * T:(t + 1) * T],
                in1=qh[:, t * T:(t + 1) * T], op=AluOpType.mult))(), "v")
        V_QSQ = c.get("v")

        wait(t_prog, "v", V_QSQ)
        evac_v = []
        for cc in range(8):
            if cc >= 1:
                wait(t_prog, "v", V_QSQ + cc)
            def mk_qn(cc=cc):
                def f(tensor):
                    mm = None
                    for t in range(4):
                        mm = tensor.matmul(psM[0:8, :], hm[:, t * 8:(t + 1) * 8],
                                           qsq[:, t * T + cc * 512: t * T + (cc + 1) * 512],
                                           start=(t == 0), stop=(t == 3))
                    return mm
                return f
            tq = run(t_prog, mk_qn(), "t")
            wait(v_prog, "t", tq)
            run(v_prog, (lambda cc=cc: lambda v: v.tensor_copy(
                out=n2[0:8, cc * 512:(cc + 1) * 512], in_=psM[0:8, :]))(), "v")
        V_QN = c.get("v")

        wait(v_prog, "t", c.get("t"))
        for t in range(1 if mqa else 4):
            run(v_prog, (lambda t=t: lambda v: v.tensor_tensor(
                out=qsq[:, t * T:(t + 1) * T], in0=kf[:, t * T:(t + 1) * T],
                in1=kf[:, t * T:(t + 1) * T], op=AluOpType.mult))(), "v")
        V_KSQ = c.get("v")

        wait(t_prog, "v", V_KSQ)
        for cc in range(8):
            if cc >= 1:
                wait(t_prog, "v", V_KSQ + cc)
            def mk_kn(cc=cc):
                def f(tensor):
                    if mqa:
                        return tensor.matmul(psM[0:8, :], hm[:, 32:40],
                                             qsq[:, cc * 512:(cc + 1) * 512],
                                             start=True, stop=True)
                    mm = None
                    for t in range(4):
                        mm = tensor.matmul(psM[0:8, :], hm[:, t * 8:(t + 1) * 8],
                                           qsq[:, t * T + cc * 512: t * T + (cc + 1) * 512],
                                           start=(t == 0), stop=(t == 3))
                    return mm
                return f
            tk = run(t_prog, mk_kn(), "t")
            wait(v_prog, "t", tk)
            run(v_prog, (lambda cc=cc: lambda v: v.tensor_copy(
                out=n2[0:8, T + cc * 512: T + (cc + 1) * 512], in_=psM[0:8, :]))(), "v")

        run(v_prog, lambda v: v.tensor_reduce(out=mx[0:8, 0:1], in_=n2[0:8, 0:T],
                                              axis=mybir.AxisListType.X, op=AluOpType.max), "v")
        run(v_prog, lambda v: v.tensor_reduce(out=mx[0:8, 1:2], in_=n2[0:8, T:2 * T],
                                              axis=mybir.AxisListType.X, op=AluOpType.max), "v")
        run(v_prog, lambda v: v.tensor_tensor(out=mx[0:8, 2:3], in0=mx[0:8, 0:1],
                                              in1=mx[0:8, 1:2], op=AluOpType.mult), "v")
        V_M2 = c.get("v")
        wait(s_prog, "v", V_M2)
        run(s_prog, lambda s: s.activation(mx[0:8, 3:4], mx[0:8, 2:3], AF.Sqrt,
                                           scale=1.0 / 64.0), "s")
        S_SQRT = c.get("s")
        wait(v_prog, "s", S_SQRT)
        run(v_prog, lambda v: v.tensor_scalar(out=i8n[:, :], in0=i8[:, :],
                                              scalar1=mx[0:8, 3:4], scalar2=-1.0,
                                              op0=AluOpType.mult, op1=AluOpType.mult), "v")
        V_I8N = c.get("v")
        wait(t_prog, "v", V_I8N)
        T_NEGM = run(t_prog, lambda t: t.matmul(psC[:, 0:8], ones8[:, :], i8n[:, :],
                                                start=True, stop=True), "t")
        wait(v_prog, "t", T_NEGM)
        run(v_prog, lambda v: v.tensor_copy(out=negm[:, :], in_=psC[:, 0:8]), "v")
        for slot, delta in ((4, 128), (5, 256), (6, 384)):
            run(v_prog, (lambda slot=slot, delta=delta: lambda v: v.memset(
                pbuf[:, slot * 512:slot * 512 + delta], 0.0))(), "v")
        PRE_V = c.get("v")
        wait(t_prog, "v", PRE_V)
        wait(s_prog, "v", PRE_V)
        _cm_qsq.__exit__(None, None, None)
        _cm_att = nc.sbuf_tensor("att", [128, 4 * T], BF16); att = _cm_att.__enter__()

        exp_done_at = {}
        pbuf_last_pv = {i: 0 for i in range(7)}
        pv_stop_at = {}
        norm_done_at = {}
        last_psC_free = 0
        sc_idx = 0
        for qb in range(NQB):
            nkt = 4 * (qb + 1)
            for hg in range(2):
                for kt in range(nkt):
                    dd = kt * 128 - qb * 512
                    for hi in range(4):
                        h = hg * 4 + hi
                        i = sc_idx
                        bank = psS[i % 2]
                        lo = max(0, dd)
                        if i >= 2:
                            wait(t_prog, "s", exp_done_at[i - 2])
                        def mk_sc(bank=bank, h=h, kt=kt, qb=qb, dd=dd, lo=lo):
                            def f(tensor):
                                if dd >= 0:
                                    tensor.matmul(bank[0:128, lo:512],
                                                  i128[:, :], tri[:, 0:512 - lo],
                                                  start=True, stop=False,
                                                  skip_group_check=True)
                                return tensor.matmul(
                                    bank[0:128, lo:512], k_lhsT(h, kt),
                                    qh[(h % 2) * 64:(h % 2) * 64 + 64,
                                       (h // 2) * T + qb * 512 + lo:(h // 2) * T + (qb + 1) * 512],
                                    start=(dd < 0), stop=True,
                                    skip_group_check=(dd >= 0))
                            return f
                        sc_t = run(t_prog, mk_sc(), "t")
                        slot = (i % 4) if dd <= 0 else 3 + dd // 128
                        wait(s_prog, "t", sc_t)
                        wait(s_prog, "t", pbuf_last_pv[slot])
                        def mk_exp(bank=bank, slot=slot, lo=lo, h=h):
                            def f(scalar):
                                return scalar.activation(
                                    pbuf[:, slot * 512 + lo:(slot + 1) * 512],
                                    bank[0:128, lo:512], AF.Exp,
                                    bias=negm[:, h:h + 1], scale=0.125)
                            return f
                        exp_done_at[i] = run(s_prog, mk_exp(), "s")
                        wait(t_prog, "s", exp_done_at[i])
                        if kt == 0:
                            pq = (qb, hg - 1) if hg == 1 else (qb - 1, 1)
                            if pq in norm_done_at:
                                wait(t_prog, "v", norm_done_at[pq])
                        def mk_pv(hi=hi, h=h, kt=kt, slot=slot,
                                  start=(kt == 0), stop=(kt == nkt - 1)):
                            def f(tensor):
                                return tensor.matmul(psPV[hi][0:65, :], va_lhsT(h, kt),
                                                     pbuf[:, slot * 512:(slot + 1) * 512],
                                                     start=start, stop=stop)
                            return f
                        pv_t = run(t_prog, mk_pv(), "t")
                        pbuf_last_pv[slot] = pv_t
                        if kt == nkt - 1:
                            pv_stop_at[(qb, hg, hi)] = pv_t
                        sc_idx += 1
                for hi in range(4):
                    wait(v_prog, "t", pv_stop_at[(qb, hg, hi)])
                    run(v_prog, (lambda hi=hi: lambda v: v.tensor_copy(
                        out=drow[:, hi * 512:(hi + 1) * 512],
                        in_=psPV[hi][64:65, :]))(), "v")
                wait(v_prog, "v", c.get("v"))
                run(v_prog, lambda v: v.reciprocal(out=drow2[:, :], in_=drow[:, :]), "v")
                wait(v_prog, "v", c.get("v"))
                run(v_prog, lambda v: v.tensor_copy(out=drowb[:, :], in_=drow2[:, :]), "v")
                V_RECIP = c.get("v")
                for pp in range(2):
                    wait(t_prog, "v", V_RECIP)
                    if last_psC_free:
                        wait(t_prog, "v", last_psC_free)
                    def mk_bc(pp=pp):
                        def f(tensor):
                            tensor.matmul(psC[0:64, :], onesf[:, :],
                                          drowb[:, (2 * pp) * 512:(2 * pp + 1) * 512],
                                          start=True, stop=True)
                            return tensor.matmul(psC[64:128, :], onesf[:, :],
                                                 drowb[:, (2 * pp + 1) * 512:(2 * pp + 2) * 512],
                                                 start=True, stop=True)
                        return f
                    bc_t = run(t_prog, mk_bc(), "t")
                    wait(v_prog, "t", bc_t)
                    run(v_prog, lambda v: v.tensor_copy(out=bco[:, :], in_=psC[:, :]), "v")
                    last_psC_free = c.get("v")
                    for sub in range(2):
                        hi = pp * 2 + sub
                        h = hg * 4 + hi
                        def mk_nm(hi=hi, h=h, sub=sub, qb=qb):
                            def f(vector):
                                return vector.tensor_tensor(
                                    out=att[(h % 2) * 64:(h % 2) * 64 + 64,
                                            (h // 2) * T + qb * 512:(h // 2) * T + (qb + 1) * 512],
                                    in0=psPV[hi][0:64, :],
                                    in1=bco[sub * 64:(sub + 1) * 64, :],
                                    op=AluOpType.mult)
                            return f
                        run(v_prog, mk_nm(), "v")
                norm_done_at[(qb, hg)] = c.get("v")

        block.tensor(lambda e: _emit(e, t_prog, sems))
        block.scalar(lambda e: _emit(e, s_prog, sems))
        block.vector(lambda e: _emit(e, v_prog, sems))
        _cm_att.__exit__(None, None, None)

        def sync_stores(sync):
            sync.wait_ge(s_v, c.get("v"))
            for t in range(4):
                sync.dma_start(out=attn_t[t * 128:(t + 1) * 128, :],
                               in_=att[:, t * T:(t + 1) * T]).then_inc(s_st, 16)
            sync.wait_ge(s_st, 64)
        block.sync(sync_stores)
    return nc


def host_attn_consts(mqa):
    tri = np.zeros((128, 512), np.float32)
    tri[:, :128] = np.where(np.arange(128)[:, None] > np.arange(128)[None, :], NEG, 0.0)
    tri = tri.astype(NP_BF16)
    i128 = np.eye(128, dtype=NP_BF16)
    hm = np.zeros((5, 128, 8), np.float32)
    for t in range(4):
        for sub in range(2):
            hm[t, sub * 64:(sub + 1) * 64, 2 * t + sub] = 1.0
    hm[4, :, :] = 0.5
    i8 = np.eye(8, dtype=np.float32)
    ones8 = np.ones((8, 128), np.float32)
    onesf = np.ones((1, 64), dtype=NP_BF16)
    return {"tri": tri, "i128": i128, "hm": hm.astype(NP_BF16), "i8": i8,
            "ones8": ones8, "onesf": onesf}


def host_vaug(v, mqa):
    nvt = 32 if mqa else 256
    out = np.zeros((128, nvt * 65), NP_BF16)
    if mqa:
        for tt in range(32):
            out[:, tt * 65:tt * 65 + 64] = v[tt * 128:(tt + 1) * 128, :]
            out[:, tt * 65 + 64] = np.float32(1.0)
    else:
        for h in range(8):
            for tt in range(32):
                i = h * 32 + tt
                out[:, i * 65:i * 65 + 64] = v[tt * 128:(tt + 1) * 128, h * 64:(h + 1) * 64]
                out[:, i * 65 + 64] = np.float32(1.0)
    return out


def run_attn(q, k, v, mqa):
    nc = _get_prog("attn_mqa" if mqa else "attn_mha", lambda: build_attn(mqa))
    consts = host_attn_consts(mqa)
    in_maps = []
    for core in range(NCORE):
        b, half = divmod(core, 2)
        rows = slice(half * 512, (half + 1) * 512)
        m = dict(consts)
        m["qh"] = np.ascontiguousarray(q[b][rows, :])
        if mqa:
            m["kf"] = np.ascontiguousarray(k[b])
            m["vaug"] = host_vaug(v[b], True)
        else:
            m["kf"] = np.ascontiguousarray(k[b][rows, :])
            m["vaug"] = host_vaug(v[b][:, half * 512:(half + 1) * 512], False)
        in_maps.append(m)
    res = run_bass_kernel_spmd(nc, in_maps, core_ids=list(range(NCORE)))
    attn = np.zeros((B, 1024, T), NP_BF16)
    for core in range(NCORE):
        b, half = divmod(core, 2)
        attn[b, half * 512:(half + 1) * 512, :] = res.results[core]["attn"]
    return attn




MAGIC = 12582912.0
EPS = 1e-5


class Prog:

    def __init__(self, nc):
        self.nc = nc
        self.c = Cnt()
        self.progs = {"t": [], "s": [], "v": []}

    def wait(self, eng, sem, val):
        if val > 0:
            self.progs[eng].append(("wait", sem, val))

    def run(self, eng, fn):
        if eng in ("s", "v") and self.c.get(eng) > 0:
            self.progs[eng].append(("wait", eng, self.c.get(eng)))
        self.c.inc(eng, 1)
        self.progs[eng].append(("op", fn, (eng, 1)))
        return self.c.get(eng)

    def emit(self, block, sems):
        block.tensor(lambda e: _emit(e, self.progs["t"], sems))
        block.scalar(lambda e: _emit(e, self.progs["s"], sems))
        block.vector(lambda e: _emit(e, self.progs["v"], sems))


def emit_ln_tail(pr, rbuf, scr, rows, bcm, bcs, xout_ap, psSum, psSq, psBc,
                 onescolb, ones1f, epsb, gcol=None, bcol=None, nchunk=8, ntc=4,
                 pre_wait=None, chunk_prewait=None, post_add_ap=None):
    c = pr.c
    CW = 512
    if pre_wait is not None:
        pr.wait("v", pre_wait[0], pre_wait[1])
    mm_done = {}
    for tc_i in range(ntc):
        for ch in range(nchunk):
            slot = ch % 2
            if (tc_i, ch) >= (0, 2) and mm_done:
                key = (tc_i, ch - 2) if ch >= 2 else (tc_i - 1, nchunk - 2 + ch)
                if key in mm_done:
                    pr.wait("v", "t", mm_done[key])
            pr.run("v", (lambda ch=ch, tc_i=tc_i, slot=slot: lambda v: v.tensor_copy(
                out=scr[:, (2 * slot) * CW:(2 * slot + 1) * CW],
                in_=rbuf[ch][:, tc_i * CW:(tc_i + 1) * CW]))())
            vd = pr.run("v", (lambda ch=ch, tc_i=tc_i, slot=slot: lambda v: v.tensor_tensor(
                out=scr[:, (2 * slot + 1) * CW:(2 * slot + 2) * CW],
                in0=rbuf[ch][:, tc_i * CW:(tc_i + 1) * CW],
                in1=rbuf[ch][:, tc_i * CW:(tc_i + 1) * CW], op=AluOpType.mult))())
            pr.wait("t", "v", vd)
            def mk_mm(ch=ch, slot=slot, first=(ch == 0), last=(ch == nchunk - 1)):
                def f(tensor):
                    tensor.matmul(psSum[0:1, :], onescolb[:, :],
                                  scr[:, (2 * slot) * CW:(2 * slot + 1) * CW],
                                  start=first, stop=last)
                    return tensor.matmul(psSq[0:1, :], onescolb[:, :],
                                         scr[:, (2 * slot + 1) * CW:(2 * slot + 2) * CW],
                                         start=first, stop=last)
                return f
            mm_done[(tc_i, ch)] = pr.run("t", mk_mm())
        pr.wait("v", "t", mm_done[(tc_i, nchunk - 1)])
        pr.run("v", (lambda tc_i=tc_i: lambda v: v.tensor_copy(
            out=rows[0][:, tc_i * CW:(tc_i + 1) * CW], in_=psSum[0:1, :]))())
        V_EV = pr.run("v", (lambda tc_i=tc_i: lambda v: v.tensor_copy(
            out=rows[1][:, tc_i * CW:(tc_i + 1) * CW], in_=psSq[0:1, :]))())
        if tc_i < ntc - 1:
            pr.wait("t", "v", V_EV)
    n = float(nchunk * 128)
    pr.run("v", lambda v: v.tensor_scalar(out=rows[0][:, :], in0=rows[0][:, :],
                                          scalar1=1.0 / n, scalar2=None,
                                          op0=AluOpType.mult))
    pr.run("v", lambda v: v.tensor_scalar(out=rows[1][:, :], in0=rows[1][:, :],
                                          scalar1=1.0 / n, scalar2=None,
                                          op0=AluOpType.mult))
    pr.run("v", lambda v: v.tensor_tensor(out=rows[2][:, :], in0=rows[0][:, :],
                                          in1=rows[0][:, :], op=AluOpType.mult))
    pr.run("v", lambda v: v.tensor_tensor(out=rows[1][:, :], in0=rows[1][:, :],
                                          in1=rows[2][:, :], op=AluOpType.subtract))
    pr.run("v", lambda v: v.memset(epsb[:, :], EPS))
    V_VAR = c.get("v")
    pr.wait("s", "v", V_VAR)
    S_SQ = pr.run("s", lambda s: s.activation(rows[2][:, :], rows[1][:, :],
                                              AF.Sqrt, bias=epsb[:, :], scale=1.0))
    pr.wait("v", "s", S_SQ)
    pr.run("v", lambda v: v.reciprocal(out=rows[3][:, :],
                                       in_=rows[2][:, :]))
    pr.run("v", lambda v: v.tensor_tensor(out=rows[2][:, :], in0=rows[0][:, :],
                                          in1=rows[3][:, :], op=AluOpType.mult))
    V_ROWS = c.get("v")
    pr.wait("t", "v", V_ROWS)
    for tc_i in range(ntc):
        td = pr.run("t", (lambda tc_i=tc_i: lambda t: t.matmul(
            psBc[0:128, :], ones1f[:, :],
            rows[3][:, tc_i * CW:(tc_i + 1) * CW],
            start=True, stop=True))())
        pr.wait("v", "t", td)
        pr.run("v", (lambda tc_i=tc_i: lambda v: v.tensor_copy(
            out=bcs[:, tc_i * CW:(tc_i + 1) * CW], in_=psBc[0:128, :]))())
        pr.wait("t", "v", c.get("v"))
        td = pr.run("t", (lambda tc_i=tc_i: lambda t: t.matmul(
            psBc[0:128, :], ones1f[:, :],
            rows[2][:, tc_i * CW:(tc_i + 1) * CW],
            start=True, stop=True))())
        pr.wait("v", "t", td)
        pr.run("v", (lambda tc_i=tc_i: lambda v: v.tensor_copy(
            out=bcm[:, tc_i * CW:(tc_i + 1) * CW], in_=psBc[0:128, :]))())
        if tc_i < ntc - 1:
            pr.wait("t", "v", c.get("v"))
    ready = []
    for ch in range(nchunk):
        if chunk_prewait is not None:
            pw = chunk_prewait(ch)
            if pw is not None:
                pr.wait("v", pw[0], pw[1])
        xo = xout_ap(ch)
        pr.run("v", (lambda ch=ch, xo=xo: lambda v: v.tensor_tensor(
            out=xo, in0=rbuf[ch][:, :], in1=bcs[:, :], op=AluOpType.mult))())
        pr.run("v", (lambda ch=ch, xo=xo: lambda v: v.tensor_tensor(
            out=xo, in0=xo, in1=bcm[:, :], op=AluOpType.subtract))())
        if gcol is not None:
            pr.run("v", (lambda ch=ch, xo=xo: lambda v: v.tensor_scalar(
                out=xo, in0=xo, scalar1=gcol[:, ch:ch + 1], scalar2=bcol[:, ch:ch + 1],
                op0=AluOpType.mult, op1=AluOpType.add))())
        if post_add_ap is not None:
            pa = post_add_ap(ch)
            pr.run("v", (lambda xo=xo, pa=pa: lambda v: v.tensor_tensor(
                out=pa, in0=xo, in1=pa, op=AluOpType.add))())
        ready.append(c.get("v"))
    return ready



def build_proj_ln(with_bias):
    nc = bass.Bass("TRN2", target_bir_lowering=False)
    aT_t = nc.dram_tensor("aT", [D, TC], BF16, kind="ExternalInput").ap()
    xT_t = nc.dram_tensor("xT", [D, TC], F32, kind="ExternalInput").ap()
    w_t = nc.dram_tensor("w", [D, D], F32, kind="ExternalInput").ap()
    if with_bias:
        bo_t = nc.dram_tensor("bo", [D], F32, kind="ExternalInput").ap()
    oc1f_t = nc.dram_tensor("oc1f", [1, 128], F32, kind="ExternalInput").ap()
    ocb_t = nc.dram_tensor("ocb", [128, 1], BF16, kind="ExternalInput").ap()
    or512_t = nc.dram_tensor("or512", [1, 512], BF16, kind="ExternalInput").ap()
    xo_t = nc.dram_tensor("xo", [D, TC], F32, kind="ExternalOutput").ap()

    c0 = Cnt()
    with ExitStack() as _st:
        borow = _st.enter_context(nc.sbuf_tensor("borow", [1, D], BF16))
        oc1f = _st.enter_context(nc.sbuf_tensor("oc1f_s", [1, 128], F32))
        ocb = _st.enter_context(nc.sbuf_tensor("ocb_s", [128, 1], BF16))
        or512 = _st.enter_context(nc.sbuf_tensor("or512_s", [1, 512], BF16))
        epsb = _st.enter_context(nc.sbuf_tensor("epsb", [1, 1], F32))
        rbuf = [_st.enter_context(nc.sbuf_tensor(f"rf{i}", [128, TC], F32)) for i in range(8)]
        _cm_ab = nc.sbuf_tensor("ab", [128, 8 * TC], BF16); ab = _cm_ab.__enter__()
        _cm_xb = nc.sbuf_tensor("xb", [128, 8 * TC], F32); xb = _cm_xb.__enter__()
        _cm_wb = nc.sbuf_tensor("wb", [128, 8 * D], BF16); wb = _cm_wb.__enter__()
        psA = _st.enter_context(nc.psum_tensor("psA", [128, 512], F32))
        psB = _st.enter_context(nc.psum_tensor("psB", [128, 512], F32))
        psSum = _st.enter_context(nc.psum_tensor("psSum", [128, 512], F32))
        psSq = _st.enter_context(nc.psum_tensor("psSq", [128, 512], F32))
        psBc = _st.enter_context(nc.psum_tensor("psBc", [128, 512], F32))
        s_dh = _st.enter_context(nc.semaphore("s_dh"))
        s_ds = _st.enter_context(nc.semaphore("s_ds"))
        s_t = _st.enter_context(nc.semaphore("s_t"))
        s_s = _st.enter_context(nc.semaphore("s_s"))
        s_v = _st.enter_context(nc.semaphore("s_v"))
        s_st = _st.enter_context(nc.semaphore("s_st"))
        block = _st.enter_context(nc.Block())
        sems = {"dh": s_dh, "ds": s_ds, "t": s_t, "s": s_s, "v": s_v, "st": s_st}

        def sync_loads(sync):
            for cc in range(8):
                sync.dma_start(out=xb[:, cc * TC:(cc + 1) * TC],
                               in_=xT_t[cc * 128:(cc + 1) * 128, :]).then_inc(s_dh, 16)
                sync.dma_start(out=ab[:, cc * TC:(cc + 1) * TC],
                               in_=aT_t[cc * 128:(cc + 1) * 128, :]).then_inc(s_dh, 16)
                c0.inc("dh", 32)
            for dst, srct in ((oc1f, oc1f_t), (ocb, ocb_t), (or512, or512_t)):
                sync.dma_start(out=dst[:, :], in_=srct[:, :]).then_inc(s_dh, 16)
                c0.inc("dh", 16)

        def gp_loads(gpsimd):
            for cc in range(8):
                gpsimd.dma_start(out=wb[:, cc * D:(cc + 1) * D],
                                 in_=w_t[cc * 128:(cc + 1) * 128, :]).then_inc(s_ds, 16)
                c0.inc("ds", 16)
            if with_bias:
                gpsimd.dma_start(out=borow[:, :],
                                 in_=bo_t.rearrange("(o h) -> o h", o=1)).then_inc(s_ds, 16)
                c0.inc("ds", 16)

        block.sync(sync_loads)
        block.gpsimd(gp_loads)

        pr = Prog(nc)
        pr.wait("t", "dh", c0.get("dh"))
        pr.wait("t", "ds", c0.get("ds"))
        gi = 0
        for oc in range(8):
            for tc_i in range(4):
                ps = psA if gi % 2 == 0 else psB
                if gi >= 2:
                    pr.wait("t", "v", gi - 1)
                def mk_mm(oc=oc, tc_i=tc_i, ps=ps):
                    def f(tensor):
                        mm = None
                        for ic in range(8):
                            mm = tensor.matmul(
                                ps[:, :], wb[:, ic * D + oc * 128: ic * D + (oc + 1) * 128],
                                ab[:, ic * TC + tc_i * 512: ic * TC + (tc_i + 1) * 512],
                                start=(ic == 0), stop=(ic == 7) and not with_bias)
                        if with_bias:
                            mm = tensor.matmul(ps[:, :],
                                               borow[:, oc * 128:(oc + 1) * 128],
                                               or512[:, :], start=False, stop=True)
                        return mm
                    return f
                td = pr.run("t", mk_mm())
                pr.wait("v", "t", td)
                pr.run("v", (lambda oc=oc, tc_i=tc_i, ps=ps: lambda v: v.tensor_tensor(
                    out=rbuf[oc][:, tc_i * 512:(tc_i + 1) * 512], in0=ps[:, :],
                    in1=xb[:, oc * TC + tc_i * 512: oc * TC + (tc_i + 1) * 512],
                    op=AluOpType.add))())
                gi += 1
        T_MM_DONE = pr.c.get("t")
        _cm_wb.__exit__(None, None, None)
        _cm_xb.__exit__(None, None, None)
        _cm_ab.__exit__(None, None, None)
        _cm_xo = nc.sbuf_tensor("xout", [128, 8 * TC], F32); xout = _cm_xo.__enter__()
        _cm_scr = nc.sbuf_tensor("scr", [128, 4 * 512], BF16); scr = _cm_scr.__enter__()
        _cm_rows = nc.sbuf_tensor("rows", [1, 4 * TC], F32); rows_b = _cm_rows.__enter__()
        rows = [rows_b[:, k * TC:(k + 1) * TC] for k in range(4)]
        _cm_bcm = nc.sbuf_tensor("bcm", [128, TC], F32); bcm = _cm_bcm.__enter__()
        _cm_bcs = nc.sbuf_tensor("bcs", [128, TC], F32); bcs = _cm_bcs.__enter__()
        emit_ln_tail(pr, rbuf, scr, rows, bcm, bcs,
                     lambda ch: xout[:, ch * TC:(ch + 1) * TC],
                     psSum, psSq, psBc, ocb, oc1f, epsb, pre_wait=("t", T_MM_DONE))
        pr.emit(block, sems)

        def sync_stores(sync):
            sync.wait_ge(s_v, pr.c.get("v"))
            for cc in range(8):
                sync.dma_start(out=xo_t[cc * 128:(cc + 1) * 128, :],
                               in_=xout[:, cc * TC:(cc + 1) * TC]).then_inc(s_st, 16)
            sync.wait_ge(s_st, 128)
        block.sync(sync_stores)
        _cm_bcs.__exit__(None, None, None)
        _cm_bcm.__exit__(None, None, None)
        _cm_rows.__exit__(None, None, None)
        _cm_scr.__exit__(None, None, None)
        _cm_xo.__exit__(None, None, None)
    return nc


def ln_consts():
    return {"oc1f": np.ones((1, 128), np.float32),
            "ocb": np.ones((128, 1), dtype=NP_BF16),
            "or512": np.ones((1, 512), dtype=NP_BF16)}


def run_proj_ln(a, x, W, bias):
    nc = _get_prog("proj_ln_b" if bias is not None else "proj_ln",
                   lambda: build_proj_ln(bias is not None))
    consts = ln_consts()
    in_maps = []
    for core in range(NCORE):
        b, half = divmod(core, 2)
        m = dict(consts)
        m["aT"] = np.ascontiguousarray(a[b][:, half * TC:(half + 1) * TC])
        m["xT"] = np.ascontiguousarray(x[b, half * TC:(half + 1) * TC, :].T)
        m["w"] = W
        if bias is not None:
            m["bo"] = bias
        in_maps.append(m)
    res = run_bass_kernel_spmd(nc, in_maps, core_ids=list(range(NCORE)))
    xn = np.zeros((B, T, D), np.float32)
    for core in range(NCORE):
        b, half = divmod(core, 2)
        xn[b, half * TC:(half + 1) * TC, :] = res.results[core]["xo"].T
    return xn





def emit_absmax_row(pr, xch, nch, sc32, scb, psT, psTR, mrow, i128, sqtmp=None):
    c = pr.c
    if nch == 1:
        pr.run("v", lambda v: v.tensor_copy(out=scb[:, 0:TC], in_=xch(0)))
    else:
        pr.run("v", lambda v: v.tensor_tensor(out=sc32[:, :], in0=xch(0), in1=xch(0),
                                              op=AluOpType.mult))
        for ch in range(1, nch):
            pr.run("v", (lambda ch=ch: lambda v: v.tensor_tensor(
                out=sqtmp[:, :], in0=xch(ch), in1=xch(ch), op=AluOpType.mult))())
            pr.run("v", lambda v: v.tensor_tensor(out=sc32[:, :], in0=sc32[:, :],
                                                  in1=sqtmp[:, :], op=AluOpType.max))
        pr.run("v", lambda v: v.tensor_copy(out=scb[:, 0:TC], in_=sc32[:, :]))
    V_AM = c.get("v")
    pr.wait("t", "v", V_AM)
    for tt in range(16):
        td = pr.run("t", (lambda tt=tt: lambda t: t.transpose(
            psT[0:128, 0:128], scb[:, tt * 128:(tt + 1) * 128], i128[:, :]))())
        pr.wait("v", "t", td)
        pr.run("v", (lambda tt=tt: lambda v: v.tensor_reduce(
            out=sc32[:, tt:tt + 1], in_=psT[0:128, 0:128],
            axis=mybir.AxisListType.X, op=AluOpType.max))())
        vd = pr.run("v", (lambda tt=tt: lambda v: v.tensor_copy(
            out=scb[:, TC + tt:TC + tt + 1], in_=sc32[:, tt:tt + 1]))())
        pr.wait("t", "v", vd)
        td2 = pr.run("t", (lambda tt=tt: lambda t: t.transpose(
            psTR[0:1, 0:128], scb[:, TC + tt:TC + tt + 1], i128[:, :]))())
        pr.wait("v", "t", td2)
        pr.run("v", (lambda tt=tt: lambda v: v.tensor_copy(
            out=mrow[:, tt * 128:(tt + 1) * 128], in_=psTR[0:1, 0:128]))())
    pr.run("v", lambda v: v.tensor_scalar(out=mrow[:, :], in0=mrow[:, :],
                                          scalar1=1e-30, scalar2=None,
                                          op0=AluOpType.max))
    V_PRE = c.get("v")
    pr.wait("s", "v", V_PRE)
    S_RT = pr.run("s", lambda s: s.activation(mrow[:, :], mrow[:, :], AF.Sqrt,
                                              scale=1.0))
    pr.wait("v", "s", S_RT)
    return c.get("v")


def emit_row_bcast(pr, row_sl, dst, lhs, psC, ntc=4, evac_cast=None):
    for tc_i in range(ntc):
        td = pr.run("t", (lambda tc_i=tc_i: lambda t: t.matmul(
            psC[0:128, :], lhs[:, :], row_sl(tc_i), start=True, stop=True))())
        pr.wait("v", "t", td)
        pr.run("v", (lambda tc_i=tc_i: lambda v: v.tensor_copy(
            out=dst[:, tc_i * 512:(tc_i + 1) * 512], in_=psC[0:128, :]))())
        pr.wait("t", "v", pr.c.get("v"))


def emit_wquant_pass1(pr, c0, wf, wsum, wcol, msc, wsc128, ocf, oc1f, psC,
                      group_sz, ng, n_el):
    pr.run("v", lambda v: v.memset(wsum[:, :], 0.0))
    absum_done = []
    for g in range(ng):
        half = g % 2
        pr.wait("v", "w%d" % half, (g // 2 + 1) * group_sz * 16)
        for cc in range(group_sz):
            pr.run("v", (lambda half=half, cc=cc: lambda v: v.tensor_reduce(
                out=wcol[:, :], in_=wf[:, (half * group_sz + cc) * 1024:
                                       (half * group_sz + cc + 1) * 1024],
                axis=mybir.AxisListType.X, op=AluOpType.add,
                apply_absolute_value=True))())
            pr.run("v", lambda v: v.tensor_tensor(out=wsum[:, :], in0=wsum[:, :],
                                                  in1=wcol[:, :], op=AluOpType.add))
        absum_done.append(pr.c.get("v"))
    pr.wait("t", "v", pr.c.get("v"))
    td = pr.run("t", lambda t: t.matmul(psC[0:1, 0:1], wsum[:, :], ocf[:, :],
                                        start=True, stop=True))
    pr.wait("v", "t", td)
    pr.run("v", lambda v: v.tensor_copy(out=msc[:, 0:1], in_=psC[0:1, 0:1]))
    pr.run("v", (lambda n_el=n_el: lambda v: v.tensor_scalar(
        out=msc[:, 1:2], in0=msc[:, 0:1], scalar1=1.0 / n_el, scalar2=1e-5,
        op0=AluOpType.mult, op1=AluOpType.max))())
    pr.run("v", lambda v: v.reciprocal(out=msc[:, 2:3], in_=msc[:, 1:2]))
    V_SC = pr.c.get("v")
    pr.wait("t", "v", V_SC)
    td = pr.run("t", lambda t: t.matmul(psC[0:128, 0:1], oc1f[:, :], msc[:, 2:3],
                                        start=True, stop=True))
    pr.wait("v", "t", td)
    pr.run("v", lambda v: v.tensor_copy(out=wsc128[:, :], in_=psC[0:128, 0:1]))
    return absum_done


def emit_ternarize_group(pr, wf, wq_ap, wsc128, half, group_sz, wtmp):
    for cc in range(group_sz):
        src_ap = wf[:, (half * group_sz + cc) * 1024:(half * group_sz + cc + 1) * 1024]
        dst = wq_ap(cc)
        pr.run("v", (lambda src_ap=src_ap: lambda v: v.tensor_scalar(
            out=wtmp[:, :], in0=src_ap, scalar1=wsc128[:, :], scalar2=1.0,
            op0=AluOpType.mult, op1=AluOpType.min))())
        pr.run("v", lambda v: v.tensor_scalar(
            out=wtmp[:, :], in0=wtmp[:, :], scalar1=-1.0, scalar2=MAGIC,
            op0=AluOpType.max, op1=AluOpType.add))
        pr.run("v", (lambda dst=dst: lambda v: v.tensor_scalar(
            out=dst, in0=wtmp[:, :], scalar1=MAGIC, scalar2=None,
            op0=AluOpType.subtract))())
    return pr.c.get("v")


def build_ffn_a():
    nc = bass.Bass("TRN2", target_bir_lowering=False)
    xT_t = nc.dram_tensor("xT", [D, TC], F32, kind="ExternalInput").ap()
    w1_t = nc.dram_tensor("w1T", [D, 4096], F32, kind="ExternalInput").ap()
    oc1f_t = nc.dram_tensor("oc1f", [1, 128], F32, kind="ExternalInput").ap()
    c127_t = nc.dram_tensor("c127", [1, 128], F32, kind="ExternalInput").ap()
    ocf_t = nc.dram_tensor("ocf", [128, 1], F32, kind="ExternalInput").ap()
    i128_t = nc.dram_tensor("i128", [128, 128], BF16, kind="ExternalInput").ap()
    h_t = nc.dram_tensor("h", [4096, TC], BF16, kind="ExternalOutput").ap()

    NG = 4
    c0 = Cnt()
    with ExitStack() as _st:
        oc1f = _st.enter_context(nc.sbuf_tensor("oc1f_s", [1, 128], F32))
        c127 = _st.enter_context(nc.sbuf_tensor("c127_s", [1, 128], F32))
        ocf = _st.enter_context(nc.sbuf_tensor("ocf_s", [128, 1], F32))
        i128 = _st.enter_context(nc.sbuf_tensor("i128_s", [128, 128], BF16))
        xq = _st.enter_context(nc.sbuf_tensor("xq", [128, 8 * TC], BF16))
        mrow = _st.enter_context(nc.sbuf_tensor("mrow", [1, TC], F32))
        rrow = _st.enter_context(nc.sbuf_tensor("rrow", [1, TC], F32))
        wsum = _st.enter_context(nc.sbuf_tensor("wsum", [128, 1], F32))
        wcol = _st.enter_context(nc.sbuf_tensor("wcol", [128, 1], F32))
        msc = _st.enter_context(nc.sbuf_tensor("msc", [1, 4], F32))
        wsc128 = _st.enter_context(nc.sbuf_tensor("wsc128", [128, 1], F32))
        bcX = _st.enter_context(nc.sbuf_tensor("bcX", [128, TC], F32))
        bcR = _st.enter_context(nc.sbuf_tensor("bcR", [128, TC], F32))
        psA = _st.enter_context(nc.psum_tensor("psA", [128, 512], F32))
        psB = _st.enter_context(nc.psum_tensor("psB", [128, 512], F32))
        psT = _st.enter_context(nc.psum_tensor("psT", [128, 1024], BF16))
        psTR = _st.enter_context(nc.psum_tensor("psTR", [128, 1024], BF16))
        psC = _st.enter_context(nc.psum_tensor("psC", [128, 512], F32))
        s_dh = _st.enter_context(nc.semaphore("s_dh"))
        s_ds = _st.enter_context(nc.semaphore("s_ds"))
        s_t = _st.enter_context(nc.semaphore("s_t"))
        s_s = _st.enter_context(nc.semaphore("s_s"))
        s_v = _st.enter_context(nc.semaphore("s_v"))
        s_st = _st.enter_context(nc.semaphore("s_st"))
        s_w0 = _st.enter_context(nc.semaphore("s_w0"))
        s_w1 = _st.enter_context(nc.semaphore("s_w1"))
        block = _st.enter_context(nc.Block())
        sems = {"dh": s_dh, "ds": s_ds, "t": s_t, "s": s_s, "v": s_v, "st": s_st,
                "w0": s_w0, "w1": s_w1}

        _cm_xb = nc.sbuf_tensor("xb", [128, 8 * TC], F32); xb = _cm_xb.__enter__()
        _cm_s32 = nc.sbuf_tensor("s32", [128, TC], F32); s32 = _cm_s32.__enter__()
        _cm_sq = nc.sbuf_tensor("sqt", [128, TC], F32); sqt = _cm_sq.__enter__()
        _cm_sb = nc.sbuf_tensor("sb", [128, TC + 16], BF16); sb = _cm_sb.__enter__()

        def sync_loads(sync):
            for cc in range(8):
                sync.dma_start(out=xb[:, cc * TC:(cc + 1) * TC],
                               in_=xT_t[cc * 128:(cc + 1) * 128, :]).then_inc(s_dh, 16)
                c0.inc("dh", 16)
            for dst, srct in ((oc1f, oc1f_t), (c127, c127_t), (ocf, ocf_t), (i128, i128_t)):
                sync.dma_start(out=dst[:, :], in_=srct[:, :]).then_inc(s_dh, 16)
                c0.inc("dh", 16)
        block.sync(sync_loads)

        pr = Prog(nc)
        pr.wait("v", "dh", c0.get("dh"))
        pr.wait("t", "dh", c0.get("dh"))
        pr.wait("s", "dh", c0.get("dh"))
        emit_absmax_row(pr, lambda ch: xb[:, ch * TC:(ch + 1) * TC], 8,
                        s32, sb, psT, psTR, mrow, i128, sqtmp=sqt)
        pr.run("v", lambda v: v.tensor_scalar(out=mrow[:, :], in0=mrow[:, :],
                                              scalar1=1e-5, scalar2=None,
                                              op0=AluOpType.max))
        pr.run("v", lambda v: v.reciprocal(out=rrow[:, :], in_=mrow[:, :]))
        pr.wait("t", "v", pr.c.get("v"))
        emit_row_bcast(pr, lambda tc_i: rrow[:, tc_i * 512:(tc_i + 1) * 512],
                       bcX, c127, psC)
        for ch in range(8):
            pr.run("v", (lambda ch=ch: lambda v: v.tensor_tensor(
                out=s32[:, :], in0=xb[:, ch * TC:(ch + 1) * TC],
                in1=bcX[:, :], op=AluOpType.mult))())
            pr.run("v", (lambda ch=ch: lambda v: v.tensor_scalar(
                out=xq[:, ch * TC:(ch + 1) * TC], in0=s32[:, :],
                scalar1=MAGIC, scalar2=MAGIC, op0=AluOpType.add,
                op1=AluOpType.subtract))())
        V_XQ = pr.c.get("v")

        _cm_sb.__exit__(None, None, None)
        _cm_sq.__exit__(None, None, None)
        _cm_s32.__exit__(None, None, None)
        _cm_xb.__exit__(None, None, None)
        _cm_wf = nc.sbuf_tensor("wf", [128, 2 * 8 * 1024], F32); wf = _cm_wf.__enter__()
        _cm_wq = nc.sbuf_tensor("wq1", [128, 8 * 1024], BF16); wq1 = _cm_wq.__enter__()
        _cm_wt = nc.sbuf_tensor("wtmp", [128, 1024], F32); wtmp = _cm_wt.__enter__()
        _cm_hb = nc.sbuf_tensor("hb", [128, 8 * TC], BF16); hb = _cm_hb.__enter__()

        marks = {"xq_done": V_XQ}
        c0.v["ds_base"] = 0
        NGS = 8

        def gp_loads(gpsimd):
            gpsimd.wait_ge(s_v, marks["xq_done"])
            for p in range(2):
                for g in range(NG):
                    half = g % 2
                    if p == 0 and g >= 2:
                        gpsimd.wait_ge(s_v, marks["absum"][g - 2])
                    elif p == 1 and g < 2:
                        gpsimd.wait_ge(s_v, marks["absum"][NG - 2 + g])
                    elif p == 1:
                        gpsimd.wait_ge(s_v, marks["tern"][g - 2])
                    sem_h = s_w0 if half == 0 else s_w1
                    for cc in range(NGS):
                        gpsimd.dma_start(
                            out=wf[:, (half * NGS + cc) * 1024:(half * NGS + cc + 1) * 1024],
                            in_=w1_t[cc * 128:(cc + 1) * 128, g * 1024:(g + 1) * 1024]
                        ).then_inc(sem_h, 16)
                        c0.inc("w%d" % half, 16)

        marks["absum"] = emit_wquant_pass1(pr, c0, wf, wsum, wcol, msc, wsc128,
                                           ocf, oc1f, psC, NGS, NG, 4096.0 * 1024.0)
        pr.run("v", lambda v: v.tensor_scalar(out=rrow[:, :], in0=mrow[:, :],
                                              scalar1=msc[:, 1:2], scalar2=1.0 / 127.0,
                                              op0=AluOpType.mult, op1=AluOpType.mult))
        pr.wait("t", "v", pr.c.get("v"))
        emit_row_bcast(pr, lambda tc_i: rrow[:, tc_i * 512:(tc_i + 1) * 512],
                       bcR, oc1f, psC)

        marks["tern"] = []
        store_ready = []
        prev_mm_last = 0
        for g in range(NG):
            half = g % 2
            pr.wait("v", "w%d" % half, (NG // 2 + g // 2 + 1) * NGS * 16)
            if g > 0:
                pr.wait("v", "t", prev_mm_last)
            tern_v = emit_ternarize_group(
                pr, wf, lambda cc: wq1[:, cc * 1024:(cc + 1) * 1024], wsc128, half,
                NGS, wtmp)
            marks["tern"].append(tern_v)
            if g > 0:
                pr.wait("t", "st", 128 * g)
            pr.wait("t", "v", tern_v)
            evac_at = []
            gi = 0
            for hc in range(8):
                for tc_i in range(4):
                    ps = psA if gi % 2 == 0 else psB
                    if gi >= 2:
                        pr.wait("t", "v", evac_at[gi - 2])
                    def mk(hc=hc, tc_i=tc_i, ps=ps):
                        def f(tensor):
                            mm = None
                            for ic in range(8):
                                mm = tensor.matmul(
                                    ps[:, :],
                                    wq1[:, ic * 1024 + hc * 128: ic * 1024 + (hc + 1) * 128],
                                    xq[:, ic * TC + tc_i * 512: ic * TC + (tc_i + 1) * 512],
                                    start=(ic == 0), stop=(ic == 7))
                            return mm
                        return f
                    td = pr.run("t", mk())
                    pr.wait("v", "t", td)
                    pr.run("v", (lambda hc=hc, tc_i=tc_i, ps=ps: lambda v: v.tensor_tensor(
                        out=hb[:, hc * TC + tc_i * 512: hc * TC + (tc_i + 1) * 512],
                        in0=ps[:, :], in1=bcR[:, tc_i * 512:(tc_i + 1) * 512],
                        op=AluOpType.mult))())
                    evac_at.append(pr.c.get("v"))
                    gi += 1
            prev_mm_last = pr.c.get("t")
            store_ready.append(pr.c.get("v"))

        pr.emit(block, sems)

        def sync_stores(sync):
            for g in range(NG):
                sync.wait_ge(s_v, store_ready[g])
                for hc in range(8):
                    sync.dma_start(out=h_t[(g * 8 + hc) * 128:(g * 8 + hc + 1) * 128, :],
                                   in_=hb[:, hc * TC:(hc + 1) * TC]).then_inc(s_st, 16)
            sync.wait_ge(s_st, 16 * 32)
        block.sync(sync_stores)
        block.gpsimd(gp_loads)
        _cm_hb.__exit__(None, None, None)
        _cm_wt.__exit__(None, None, None)
        _cm_wq.__exit__(None, None, None)
        _cm_wf.__exit__(None, None, None)
    return nc




def build_ffn_b():
    nc = bass.Bass("TRN2", target_bir_lowering=False)
    hE_t = nc.dram_tensor("hE", [4096, TC + 2], BF16, kind="ExternalInput").ap()
    xT_t = nc.dram_tensor("xT", [D, TC], F32, kind="ExternalInput").ap()
    w2_t = nc.dram_tensor("w2T", [4096, D], F32, kind="ExternalInput").ap()
    cw_t = nc.dram_tensor("cw", [4096, 3], F32, kind="ExternalInput").ap()
    cb_t = nc.dram_tensor("cb", [4096], F32, kind="ExternalInput").ap()
    oc1f_t = nc.dram_tensor("oc1f", [1, 128], F32, kind="ExternalInput").ap()
    c127_t = nc.dram_tensor("c127", [1, 128], F32, kind="ExternalInput").ap()
    ocf_t = nc.dram_tensor("ocf", [128, 1], F32, kind="ExternalInput").ap()
    i128_t = nc.dram_tensor("i128", [128, 128], BF16, kind="ExternalInput").ap()
    ocb_t = nc.dram_tensor("ocb", [128, 1], BF16, kind="ExternalInput").ap()
    or512_t = nc.dram_tensor("or512", [1, 512], BF16, kind="ExternalInput").ap()
    bd_t = nc.dram_tensor("bdec", [D], F32, kind="ExternalInput").ap()
    xo_t = nc.dram_tensor("xo", [D, TC], F32, kind="ExternalOutput").ap()
    csx = nc.dram_tensor("csx", [4096, TC], BF16)
    qx = nc.dram_tensor("qx", [4096, TC], BF16)

    NG, NGS = 4, 8
    c0 = Cnt()
    with ExitStack() as _st:
        oc1f = _st.enter_context(nc.sbuf_tensor("oc1f_s", [1, 128], F32))
        c127 = _st.enter_context(nc.sbuf_tensor("c127_s", [1, 128], F32))
        ocf = _st.enter_context(nc.sbuf_tensor("ocf_s", [128, 1], F32))
        i128 = _st.enter_context(nc.sbuf_tensor("i128_s", [128, 128], BF16))
        ocb = _st.enter_context(nc.sbuf_tensor("ocb_s", [128, 1], BF16))
        or512 = _st.enter_context(nc.sbuf_tensor("or512_s", [1, 512], BF16))
        epsb = _st.enter_context(nc.sbuf_tensor("epsb", [1, 1], F32))
        cwcol = _st.enter_context(nc.sbuf_tensor("cwcol", [128, 96], F32))
        cbcol = _st.enter_context(nc.sbuf_tensor("cbcol", [128, 32], F32))
        bdrow = _st.enter_context(nc.sbuf_tensor("bdrow", [1, D], BF16))
        wsum = _st.enter_context(nc.sbuf_tensor("wsum", [128, 1], F32))
        wcol = _st.enter_context(nc.sbuf_tensor("wcol", [128, 1], F32))
        msc = _st.enter_context(nc.sbuf_tensor("msc", [1, 4], F32))
        wsc128 = _st.enter_context(nc.sbuf_tensor("wsc128", [128, 1], F32))
        mrow = _st.enter_context(nc.sbuf_tensor("mrow", [1, TC], F32))
        rrow = _st.enter_context(nc.sbuf_tensor("rrow", [1, TC], F32))
        w2q = _st.enter_context(nc.sbuf_tensor("w2q", [128, 32 * 1024], BF16))
        psA = _st.enter_context(nc.psum_tensor("psA", [128, 512], F32))
        psB = _st.enter_context(nc.psum_tensor("psB", [128, 512], F32))
        psT = _st.enter_context(nc.psum_tensor("psT", [128, 1024], BF16))
        psTR = _st.enter_context(nc.psum_tensor("psTR", [128, 1024], BF16))
        psC = _st.enter_context(nc.psum_tensor("psC", [128, 512], F32))
        psSum = _st.enter_context(nc.psum_tensor("psSum", [128, 512], F32))
        psSq = _st.enter_context(nc.psum_tensor("psSq", [128, 512], F32))
        s_dh = _st.enter_context(nc.semaphore("s_dh"))
        s_ds = _st.enter_context(nc.semaphore("s_ds"))
        s_t = _st.enter_context(nc.semaphore("s_t"))
        s_s = _st.enter_context(nc.semaphore("s_s"))
        s_v = _st.enter_context(nc.semaphore("s_v"))
        s_st = _st.enter_context(nc.semaphore("s_st"))
        s_w0 = _st.enter_context(nc.semaphore("s_w0"))
        s_w1 = _st.enter_context(nc.semaphore("s_w1"))
        s_h0 = _st.enter_context(nc.semaphore("s_h0"))
        s_h1 = _st.enter_context(nc.semaphore("s_h1"))
        s_c0 = _st.enter_context(nc.semaphore("s_c0"))
        s_c1 = _st.enter_context(nc.semaphore("s_c1"))
        block = _st.enter_context(nc.Block())
        sems = {"dh": s_dh, "ds": s_ds, "t": s_t, "s": s_s, "v": s_v, "st": s_st,
                "w0": s_w0, "w1": s_w1, "h0": s_h0, "h1": s_h1,
                "c0": s_c0, "c1": s_c1}
        pr = Prog(nc)

        def sync_loads0(sync):
            for dst, srct in ((oc1f, oc1f_t), (c127, c127_t), (ocf, ocf_t),
                              (i128, i128_t), (ocb, ocb_t), (or512, or512_t)):
                sync.dma_start(out=dst[:, :], in_=srct[:, :]).then_inc(s_dh, 16)
                c0.inc("dh", 16)
            for hc in range(32):
                sync.dma_start(out=cwcol[:, 3 * hc:3 * (hc + 1)],
                               in_=cw_t[hc * 128:(hc + 1) * 128, :]).then_inc(s_dh, 16)
                sync.dma_start(out=cbcol[:, hc:hc + 1],
                               in_=cb_t[hc * 128:(hc + 1) * 128].rearrange("(p o) -> p o", o=1)
                               ).then_inc(s_dh, 16)
                c0.inc("dh", 32)
        block.sync(sync_loads0)
        DH0 = c0.get("dh")

        _cm_wf = nc.sbuf_tensor("wf", [128, 2 * NGS * 1024], F32); wf = _cm_wf.__enter__()
        _cm_wt = nc.sbuf_tensor("wtmp", [128, 1024], F32); wtmp = _cm_wt.__enter__()
        marks = {}

        def gp_loads(gpsimd):
            gpsimd.dma_start(out=bdrow[:, :], in_=bd_t.rearrange("(o h) -> o h", o=1)
                             ).then_inc(s_ds, 16)
            c0.inc("ds", 16)
            for p in range(2):
                for g in range(NG):
                    half = g % 2
                    if p == 0 and g >= 2:
                        gpsimd.wait_ge(s_v, marks["absum"][g - 2])
                    elif p == 1 and g < 2:
                        gpsimd.wait_ge(s_v, marks["absum"][NG - 2 + g])
                    elif p == 1:
                        gpsimd.wait_ge(s_v, marks["tern"][g - 2])
                    sem_h = s_w0 if half == 0 else s_w1
                    for cc in range(NGS):
                        gpsimd.dma_start(
                            out=wf[:, (half * NGS + cc) * 1024:(half * NGS + cc + 1) * 1024],
                            in_=w2_t[(g * NGS + cc) * 128:(g * NGS + cc + 1) * 128, :]
                        ).then_inc(sem_h, 16)
                        c0.inc("w%d" % half, 16)

        pr.wait("t", "dh", DH0)
        marks["absum"] = emit_wquant_pass1(pr, c0, wf, wsum, wcol, msc, wsc128,
                                           ocf, oc1f, psC, NGS, NG, 4096.0 * 1024.0)
        marks["tern"] = []
        for g in range(NG):
            half = g % 2
            pr.wait("v", "w%d" % half, (NG // 2 + g // 2 + 1) * NGS * 16)
            tern_v = emit_ternarize_group(
                pr, wf, lambda cc, g=g: w2q[:, (g * NGS + cc) * 1024:(g * NGS + cc + 1) * 1024],
                wsc128, half, NGS, wtmp)
            marks["tern"].append(tern_v)
        V_W2Q = pr.c.get("v")

        _cm_wt.__exit__(None, None, None)
        _cm_wf.__exit__(None, None, None)
        _cm_heb = nc.sbuf_tensor("heb", [128, 2 * (TC + 2)], BF16); heb = _cm_heb.__enter__()
        _cm_t0 = nc.sbuf_tensor("t0", [128, TC], F32); t0 = _cm_t0.__enter__()
        _cm_t1 = nc.sbuf_tensor("t1", [128, TC], F32); t1 = _cm_t1.__enter__()
        _cm_csb = nc.sbuf_tensor("csb", [128, 2 * TC], BF16); csb = _cm_csb.__enter__()
        _cm_am = nc.sbuf_tensor("am", [128, TC], F32); am = _cm_am.__enter__()
        _cm_s32 = nc.sbuf_tensor("s32", [128, TC], F32); s32 = _cm_s32.__enter__()
        _cm_scb = nc.sbuf_tensor("scb", [128, TC + 16], BF16); scb = _cm_scb.__enter__()
        _cm_bcX = nc.sbuf_tensor("bcX", [128, TC], F32); bcX = _cm_bcX.__enter__()

        conv_done = {}
        silu_done = {}
        quant_done = {}
        pr.run("v", lambda v: v.memset(am[:, :], 0.0))
        pr.wait("v", "dh", DH0)
        pr.wait("s", "dh", DH0)

        def sync_loads_h(sync):
            sync.wait_ge(s_v, V_W2Q)
            for hc in range(32):
                par = hc % 2
                if hc >= 2:
                    sync.wait_ge(s_v, conv_done[hc - 2])
                sync.dma_start(out=heb[:, par * (TC + 2):(par + 1) * (TC + 2)],
                               in_=hE_t[hc * 128:(hc + 1) * 128, :]
                               ).then_inc(s_h0 if par == 0 else s_h1, 16)
                c0.inc("h%d" % par, 16)
                sync.wait_ge(s_s, silu_done[hc])
                sync.dma_start(out=csx[hc * 128:(hc + 1) * 128, :].ap() if False else csx.ap()[hc * 128:(hc + 1) * 128, :],
                               in_=csb[:, par * TC:(par + 1) * TC]
                               ).then_inc(s_c0 if par == 0 else s_c1, 16)
                c0.inc("c%d" % par, 16)

        for hc in range(32):
            par = hc % 2
            he = heb[:, par * (TC + 2):(par + 1) * (TC + 2)]
            pr.wait("v", "h%d" % par, (hc // 2 + 1) * 16)
            pr.run("v", (lambda he=he, hc=hc: lambda v: v.tensor_scalar(
                out=t0[:, :], in0=he[:, 0:TC], scalar1=cwcol[:, 3 * hc:3 * hc + 1],
                scalar2=cbcol[:, hc:hc + 1], op0=AluOpType.mult, op1=AluOpType.add))())
            pr.run("v", (lambda he=he, hc=hc: lambda v: v.scalar_tensor_tensor(
                out=t1[:, :], in0=he[:, 1:TC + 1], scalar=cwcol[:, 3 * hc + 1:3 * hc + 2],
                in1=t0[:, :], op0=AluOpType.mult, op1=AluOpType.add))())
            vd = pr.run("v", (lambda he=he, hc=hc: lambda v: v.scalar_tensor_tensor(
                out=t0[:, :], in0=he[:, 2:TC + 2], scalar=cwcol[:, 3 * hc + 2:3 * hc + 3],
                in1=t1[:, :], op0=AluOpType.mult, op1=AluOpType.add))())
            conv_done[hc] = vd
            pr.wait("s", "v", vd)
            if hc >= 2:
                pr.wait("s", "c%d" % par, (hc // 2) * 16)
            sd = pr.run("s", (lambda par=par: lambda s: s.activation(
                csb[:, par * TC:(par + 1) * TC], t0[:, :], AF.Silu, scale=1.0))())
            silu_done[hc] = sd
            pr.wait("v", "s", sd)
            pr.run("v", (lambda par=par: lambda v: v.tensor_tensor(
                out=s32[:, :], in0=csb[:, par * TC:(par + 1) * TC],
                in1=csb[:, par * TC:(par + 1) * TC], op=AluOpType.mult))())
            pr.run("v", lambda v: v.tensor_tensor(out=am[:, :], in0=am[:, :],
                                                  in1=s32[:, :], op=AluOpType.max))

        emit_absmax_row(pr, lambda ch: am[:, :], 1, s32, scb, psT, psTR, mrow, i128)
        pr.run("v", lambda v: v.tensor_scalar(out=mrow[:, :], in0=mrow[:, :],
                                              scalar1=1e-5, scalar2=None,
                                              op0=AluOpType.max))
        pr.run("v", lambda v: v.reciprocal(out=rrow[:, :], in_=mrow[:, :]))
        pr.wait("t", "v", pr.c.get("v"))
        pr.wait("t", "dh", DH0)
        emit_row_bcast(pr, lambda tc_i: rrow[:, tc_i * 512:(tc_i + 1) * 512],
                       bcX, c127, psC)
        pr.run("v", lambda v: v.tensor_scalar(out=rrow[:, :], in0=mrow[:, :],
                                              scalar1=msc[:, 1:2], scalar2=1.0 / 127.0,
                                              op0=AluOpType.mult, op1=AluOpType.mult))

        def sync_loads_q(sync):
            sync.wait_ge(s_c0, 256)
            sync.wait_ge(s_c1, 256)
            for hc in range(32):
                par = hc % 2
                if hc >= 2:
                    sync.wait_ge(s_c0 if par == 0 else s_c1, (16 + hc // 2) * 16)
                sync.dma_start(out=csb[:, par * TC:(par + 1) * TC],
                               in_=csx.ap()[hc * 128:(hc + 1) * 128, :]
                               ).then_inc(s_h0 if par == 0 else s_h1, 16)
                c0.inc("h%d" % par, 16)
                sync.wait_ge(s_v, quant_done[hc])
                sync.dma_start(out=qx.ap()[hc * 128:(hc + 1) * 128, :],
                               in_=scb[:, par * TC:(par + 1) * TC] if False else
                               csb[:, par * TC:(par + 1) * TC]
                               ).then_inc(s_c0 if par == 0 else s_c1, 16)
                c0.inc("c%d" % par, 16)

        for hc in range(32):
            par = hc % 2
            pr.wait("v", "h%d" % par, (16 + hc // 2 + 1) * 16)
            if hc >= 2:
                pr.wait("v", "c%d" % par, (16 + hc // 2) * 16)
            pr.run("v", (lambda par=par: lambda v: v.tensor_tensor(
                out=s32[:, :], in0=csb[:, par * TC:(par + 1) * TC], in1=bcX[:, :],
                op=AluOpType.mult))())
            qd = pr.run("v", (lambda par=par: lambda v: v.tensor_scalar(
                out=csb[:, par * TC:(par + 1) * TC], in0=s32[:, :],
                scalar1=MAGIC, scalar2=MAGIC, op0=AluOpType.add,
                op1=AluOpType.subtract))())
            quant_done[hc] = qd

        V_Q_ALL = pr.c.get("v")
        _cm_bcX.__exit__(None, None, None)
        _cm_scb.__exit__(None, None, None)
        _cm_s32.__exit__(None, None, None)
        _cm_am.__exit__(None, None, None)
        _cm_csb.__exit__(None, None, None)
        _cm_t1.__exit__(None, None, None)
        _cm_t0.__exit__(None, None, None)
        _cm_heb.__exit__(None, None, None)
        _cm_xb = nc.sbuf_tensor("xb", [128, 8 * TC], F32); xb = _cm_xb.__enter__()
        _cm_bcR = nc.sbuf_tensor("bcR", [128, TC], F32); bcR = _cm_bcR.__enter__()
        _cm_xqt = nc.sbuf_tensor("xqt", [128, 32 * 512], BF16); xqt = _cm_xqt.__enter__()
        _cm_se = nc.sbuf_tensor("sevac", [128, 512], F32); sevac = _cm_se.__enter__()

        pr.wait("t", "v", V_Q_ALL)
        emit_row_bcast(pr, lambda tc_i: rrow[:, tc_i * 512:(tc_i + 1) * 512],
                       bcR, oc1f, psC)

        def sync_loads_x(sync):
            sync.wait_ge(s_c0, 512)
            sync.wait_ge(s_c1, 512)
            sync.wait_ge(s_v, quant_done[31])
            for cc in range(8):
                sync.dma_start(out=xb[:, cc * TC:(cc + 1) * TC],
                               in_=xT_t[cc * 128:(cc + 1) * 128, :]).then_inc(s_dh, 16)
                c0.inc("dh", 16)
        DH1 = DH0 + 128

        mm_done_tc = {}
        def sync_loads_xq(sync):
            for tc_i in range(4):
                par = tc_i % 2
                if tc_i >= 1:
                    sync.wait_ge(s_t, mm_done_tc[tc_i - 1])
                for hc in range(32):
                    sync.dma_start(
                        out=xqt[:, hc * 512:(hc + 1) * 512],
                        in_=qx.ap()[hc * 128:(hc + 1) * 128, tc_i * 512:(tc_i + 1) * 512]
                    ).then_inc(s_h0 if par == 0 else s_h1, 16)
                    c0.inc("h%d" % par, 16)

        rbuf = [xb[:, oc * TC:(oc + 1) * TC] for oc in range(8)]
        HQ_BASE = {0: 512, 1: 512}
        evac_marks = []
        pr.wait("t", "ds", 16)
        for tc_i in range(4):
            par = tc_i % 2
            pr.wait("t", "h%d" % par, HQ_BASE[par] + (tc_i // 2 + 1) * 512)
            pr.wait("t", "dh", DH1)
            for oc in range(8):
                gi = tc_i * 8 + oc
                ps = psA if gi % 2 == 0 else psB
                if gi >= 2:
                    pr.wait("t", "v", evac_marks[gi - 2])
                def mkmm(oc=oc, ps=ps):
                    def f(tensor):
                        for hc in range(32):
                            tensor.matmul(
                                ps[:, :],
                                w2q[:, hc * 1024 + oc * 128: hc * 1024 + (oc + 1) * 128],
                                xqt[:, hc * 512:(hc + 1) * 512],
                                start=(hc == 0), stop=False)
                        return tensor.matmul(ps[:, :], bdrow[:, oc * 128:(oc + 1) * 128],
                                             or512[:, :], start=False, stop=True)
                    return f
                td = pr.run("t", mkmm())
                pr.wait("v", "t", td)
                pr.run("v", (lambda ps=ps, tc_i=tc_i: lambda v: v.tensor_tensor(
                    out=sevac[:, :], in0=ps[:, :], in1=bcR[:, tc_i * 512:(tc_i + 1) * 512],
                    op=AluOpType.mult))())
                pr.run("v", (lambda oc=oc, tc_i=tc_i: lambda v: v.tensor_tensor(
                    out=xb[:, oc * TC + tc_i * 512: oc * TC + (tc_i + 1) * 512],
                    in0=sevac[:, :],
                    in1=xb[:, oc * TC + tc_i * 512: oc * TC + (tc_i + 1) * 512],
                    op=AluOpType.add))())
                evac_marks.append(pr.c.get("v"))
            mm_done_tc[tc_i] = pr.c.get("t")

        _cm_se.__exit__(None, None, None)
        _cm_xqt.__exit__(None, None, None)
        _cm_bcR.__exit__(None, None, None)
        _cm_scr = nc.sbuf_tensor("scr", [128, 4 * 512], BF16); scr = _cm_scr.__enter__()
        _cm_r2 = nc.sbuf_tensor("rows2", [1, 2 * TC], F32); rows2 = _cm_r2.__enter__()
        _cm_bcm = nc.sbuf_tensor("bcm", [128, TC], F32); bcm = _cm_bcm.__enter__()
        _cm_bcs = nc.sbuf_tensor("bcs", [128, TC], F32); bcs = _cm_bcs.__enter__()
        _cm_xo2 = nc.sbuf_tensor("xo2", [128, TC], F32); xo2 = _cm_xo2.__enter__()
        _cm_lg = nc.sbuf_tensor("lgb", [128, 16], F32); lgb = _cm_lg.__enter__()
        lng_t = nc.dram_tensor("lng", [D], F32, kind="ExternalInput").ap()
        lnb_t = nc.dram_tensor("lnb", [D], F32, kind="ExternalInput").ap()

        def sync_loads_ln(sync):
            for cc in range(8):
                sync.dma_start(out=lgb[:, cc:cc + 1],
                               in_=lng_t[cc * 128:(cc + 1) * 128].rearrange("(p o) -> p o", o=1)
                               ).then_inc(s_dh, 16)
                sync.dma_start(out=lgb[:, 8 + cc:9 + cc],
                               in_=lnb_t[cc * 128:(cc + 1) * 128].rearrange("(p o) -> p o", o=1)
                               ).then_inc(s_dh, 16)
                c0.inc("dh", 32)
        DH2 = DH1 + 256
        pr.wait("v", "dh", DH2)

        rows = [rows2[:, 0:TC], rows2[:, TC:2 * TC], mrow[:, :], rrow[:, :]]
        st_marks = {}
        def chunk_prewait(ch):
            if ch >= 1:
                return ("st", 16 * ch)
            return None
        ready = emit_ln_tail(pr, rbuf, scr, rows, bcm, bcs,
                             lambda ch: xo2[:, :],
                             psSum, psSq, psC, ocb, oc1f, epsb,
                             gcol=lgb[:, 0:8], bcol=lgb[:, 8:16],
                             pre_wait=("t", pr.c.get("t")),
                             chunk_prewait=chunk_prewait)
        pr.emit(block, sems)

        def sync_stores(sync):
            for ch in range(8):
                sync.wait_ge(s_v, ready[ch])
                sync.dma_start(out=xo_t[ch * 128:(ch + 1) * 128, :],
                               in_=xo2[:, :]).then_inc(s_st, 16)
            sync.wait_ge(s_st, 128)

        block.sync(sync_loads_h)
        block.sync(sync_loads_q)
        block.sync(sync_loads_x)
        block.sync(sync_loads_xq)
        block.sync(sync_loads_ln)
        block.sync(sync_stores)
        block.gpsimd(gp_loads)
        _cm_lg.__exit__(None, None, None)
        _cm_xo2.__exit__(None, None, None)
        _cm_bcs.__exit__(None, None, None)
        _cm_bcm.__exit__(None, None, None)
        _cm_r2.__exit__(None, None, None)
        _cm_scr.__exit__(None, None, None)
        _cm_xb.__exit__(None, None, None)
    return nc




def build_l1_qkv():
    nc = bass.Bass("TRN2", target_bir_lowering=False)
    xT_t = nc.dram_tensor("xT", [D, TC], F32, kind="ExternalInput").ap()
    w_t = nc.dram_tensor("wqkv", [D, 3 * D], F32, kind="ExternalInput").ap()
    q_t = nc.dram_tensor("q", [D, TC], BF16, kind="ExternalOutput").ap()
    k_t = nc.dram_tensor("k", [D, TC], BF16, kind="ExternalOutput").ap()
    v_t = nc.dram_tensor("v", [TC, D], BF16, kind="ExternalOutput").ap()
    c0 = Cnt()
    with ExitStack() as _st:
        xb = _st.enter_context(nc.sbuf_tensor("xb", [128, 8 * TC], BF16))
        wb = _st.enter_context(nc.sbuf_tensor("wb", [128, 8 * 3 * D], BF16))
        qs = _st.enter_context(nc.sbuf_tensor("qs", [128, 8 * TC], BF16))
        ks = _st.enter_context(nc.sbuf_tensor("ks", [128, 8 * TC], BF16))
        vs = _st.enter_context(nc.sbuf_tensor("vs", [128, 16 * D], BF16))
        psA = _st.enter_context(nc.psum_tensor("psA", [128, 512], F32))
        psB = _st.enter_context(nc.psum_tensor("psB", [128, 512], F32))
        s_ds = _st.enter_context(nc.semaphore("s_ds"))
        s_t = _st.enter_context(nc.semaphore("s_t"))
        s_s = _st.enter_context(nc.semaphore("s_s"))
        s_v = _st.enter_context(nc.semaphore("s_v"))
        s_st = _st.enter_context(nc.semaphore("s_st"))
        block = _st.enter_context(nc.Block())
        sems = {"ds": s_ds, "t": s_t, "s": s_s, "v": s_v, "st": s_st}

        def gp_loads(gpsimd):
            for cc in range(8):
                gpsimd.dma_start(out=xb[:, cc * TC:(cc + 1) * TC],
                                 in_=xT_t[cc * 128:(cc + 1) * 128, :]).then_inc(s_ds, 16)
                c0.inc("ds", 16)
            for cc in range(8):
                gpsimd.dma_start(out=wb[:, cc * 3 * D:(cc + 1) * 3 * D],
                                 in_=w_t[cc * 128:(cc + 1) * 128, :]).then_inc(s_ds, 16)
                c0.inc("ds", 16)
        block.gpsimd(gp_loads)

        pr = Prog(nc)
        pr.wait("t", "ds", c0.get("ds"))
        evs = []
        gi = 0
        for which, outb in ((0, qs), (1, ks)):
            for oc in range(8):
                for tci in range(4):
                    ps = psA if gi % 2 == 0 else psB
                    if gi >= 2:
                        pr.wait("t", "v", evs[gi - 2])
                    def mk(which=which, oc=oc, tci=tci, ps=ps):
                        def f(tensor):
                            mm = None
                            for ic in range(8):
                                off = ic * 3 * D + which * D + oc * 128
                                mm = tensor.matmul(
                                    ps[:, :], wb[:, off:off + 128],
                                    xb[:, ic * TC + tci * 512: ic * TC + (tci + 1) * 512],
                                    start=(ic == 0), stop=(ic == 7))
                            return mm
                        return f
                    td = pr.run("t", mk())
                    pr.wait("v", "t", td)
                    pr.run("v", (lambda outb=outb, oc=oc, tci=tci, ps=ps: lambda v: v.tensor_copy(
                        out=outb[:, oc * TC + tci * 512: oc * TC + (tci + 1) * 512],
                        in_=ps[:, :]))())
                    evs.append(pr.c.get("v"))
                    gi += 1
        for tt in range(16):
            for vh in range(2):
                ps = psA if gi % 2 == 0 else psB
                if gi >= 2:
                    pr.wait("t", "v", evs[gi - 2])
                def mkv(tt=tt, vh=vh, ps=ps):
                    def f(tensor):
                        mm = None
                        for ic in range(8):
                            off = ic * 3 * D + 2 * D + vh * 512
                            mm = tensor.matmul(
                                ps[:, :],
                                xb[:, ic * TC + tt * 128: ic * TC + (tt + 1) * 128],
                                wb[:, off:off + 512],
                                start=(ic == 0), stop=(ic == 7))
                        return mm
                    return f
                td = pr.run("t", mkv())
                pr.wait("v", "t", td)
                pr.run("v", (lambda tt=tt, vh=vh, ps=ps: lambda v: v.tensor_copy(
                    out=vs[:, tt * D + vh * 512: tt * D + (vh + 1) * 512],
                    in_=ps[:, :]))())
                evs.append(pr.c.get("v"))
                gi += 1
        pr.emit(block, sems)

        def sync_stores(sync):
            sync.wait_ge(s_v, pr.c.get("v"))
            n = 0
            for cc in range(8):
                sync.dma_start(out=q_t[cc * 128:(cc + 1) * 128, :],
                               in_=qs[:, cc * TC:(cc + 1) * TC]).then_inc(s_st, 16)
                sync.dma_start(out=k_t[cc * 128:(cc + 1) * 128, :],
                               in_=ks[:, cc * TC:(cc + 1) * TC]).then_inc(s_st, 16)
                n += 32
            for tt in range(16):
                sync.dma_start(out=v_t[tt * 128:(tt + 1) * 128, :],
                               in_=vs[:, tt * D:(tt + 1) * D]).then_inc(s_st, 16)
                n += 16
            sync.wait_ge(s_st, n)
        block.sync(sync_stores)
    return nc


def run_l1_qkv(x, wqkv):
    nc = _get_prog("l1_qkv", build_l1_qkv)
    in_maps = []
    for core in range(NCORE):
        b, half = divmod(core, 2)
        in_maps.append({"xT": np.ascontiguousarray(x[b, half * TC:(half + 1) * TC, :].T),
                        "wqkv": wqkv})
    res = run_bass_kernel_spmd(nc, in_maps, core_ids=list(range(NCORE)))
    q = np.zeros((B, D, T), NP_BF16)
    k = np.zeros((B, D, T), NP_BF16)
    v = np.zeros((B, T, D), NP_BF16)
    for core in range(NCORE):
        b, half = divmod(core, 2)
        sl = slice(half * TC, (half + 1) * TC)
        q[b, :, sl] = res.results[core]["q"]
        k[b, :, sl] = res.results[core]["k"]
        v[b, sl, :] = res.results[core]["v"]
    return q, k, v


def run_l0_ffn(x1, w1, conv_w, conv_b, w2):
    consts = {
        "oc1f": np.ones((1, 128), np.float32),
        "c127": np.full((1, 128), 127.0, np.float32),
        "ocf": np.ones((128, 1), np.float32),
        "i128": np.eye(128, dtype=NP_BF16),
    }
    nc_a = _get_prog("ffn_a", build_ffn_a)
    w1T = np.ascontiguousarray(w1.T)
    in_maps = []
    for core in range(NCORE):
        b, half = divmod(core, 2)
        m = dict(consts)
        m["xT"] = np.ascontiguousarray(x1[b, half * TC:(half + 1) * TC, :].T)
        m["w1T"] = w1T
        in_maps.append(m)
    res = run_bass_kernel_spmd(nc_a, in_maps, core_ids=list(range(NCORE)))
    hs = [res.results[c]["h"] for c in range(NCORE)]

    nc_b = _get_prog("ffn_b", build_ffn_b)
    w2T = np.ascontiguousarray(w2.T)
    cwf = np.ascontiguousarray(conv_w[:, 0, :])
    in_maps = []
    for core in range(NCORE):
        b, half = divmod(core, 2)
        hE = np.zeros((4096, TC + 2), NP_BF16)
        hE[:, 1:TC + 1] = hs[core]
        if half == 1:
            hE[:, 0] = hs[2 * b][:, TC - 1]
        if half == 0:
            hE[:, TC + 1] = hs[2 * b + 1][:, 0]
        m = dict(consts)
        m.update({"hE": hE,
                  "xT": np.ascontiguousarray(x1[b, half * TC:(half + 1) * TC, :].T),
                  "w2T": w2T, "cw": cwf, "cb": conv_b,
                  "ocb": np.ones((128, 1), dtype=NP_BF16),
                  "or512": np.ones((1, 512), dtype=NP_BF16),
                  "bdec": np.zeros(D, np.float32),
                  "lng": np.ones(D, np.float32),
                  "lnb": np.zeros(D, np.float32)})
        in_maps.append(m)
    res = run_bass_kernel_spmd(nc_b, in_maps, core_ids=list(range(NCORE)))
    x2 = np.zeros((B, T, D), np.float32)
    for core in range(NCORE):
        b, half = divmod(core, 2)
        x2[b, half * TC:(half + 1) * TC, :] = res.results[core]["xo"].T
    return x2




def build_l1_ffn():
    nc = bass.Bass("TRN2", target_bir_lowering=False)
    H, M, E, EH = 2048, 64, 8, 1024
    xT_t = nc.dram_tensor("xT", [D, TC], F32, kind="ExternalInput").ap()
    wenc_t = nc.dram_tensor("wenc", [D, H], F32, kind="ExternalInput").ap()
    benc_t = nc.dram_tensor("benc", [H], F32, kind="ExternalInput").ap()
    ws2c_t = nc.dram_tensor("ws2c", [H, M], F32, kind="ExternalInput").ap()
    bs2c_t = nc.dram_tensor("bs2c", [M], F32, kind="ExternalInput").ap()
    wr_t = nc.dram_tensor("wr", [M, E], F32, kind="ExternalInput").ap()
    br_t = nc.dram_tensor("br", [E], F32, kind="ExternalInput").ap()
    w1e_t = nc.dram_tensor("w1e", [E, M, EH], F32, kind="ExternalInput").ap()
    b1e_t = nc.dram_tensor("b1e", [E, EH], F32, kind="ExternalInput").ap()
    w2e_t = nc.dram_tensor("w2e", [E, EH, M], F32, kind="ExternalInput").ap()
    b2e_t = nc.dram_tensor("b2e", [E, M], F32, kind="ExternalInput").ap()
    wc2s_t = nc.dram_tensor("wc2s", [M, H], F32, kind="ExternalInput").ap()
    bc2s_t = nc.dram_tensor("bc2s", [H], F32, kind="ExternalInput").ap()
    wdec_t = nc.dram_tensor("wdec", [H, D], F32, kind="ExternalInput").ap()
    bdec_t = nc.dram_tensor("bdec", [D], F32, kind="ExternalInput").ap()
    lng_t = nc.dram_tensor("lng", [D], F32, kind="ExternalInput").ap()
    lnb_t = nc.dram_tensor("lnb", [D], F32, kind="ExternalInput").ap()
    oc1f_t = nc.dram_tensor("oc1f", [1, 128], F32, kind="ExternalInput").ap()
    ocb_t = nc.dram_tensor("ocb", [128, 1], BF16, kind="ExternalInput").ap()
    oc64_t = nc.dram_tensor("oc64", [1, 64], BF16, kind="ExternalInput").ap()
    or512_t = nc.dram_tensor("or512", [1, 512], BF16, kind="ExternalInput").ap()
    i128_t = nc.dram_tensor("i128", [128, 128], BF16, kind="ExternalInput").ap()
    xo_t = nc.dram_tensor("xo", [D, TC], F32, kind="ExternalOutput").ap()
    gdram = nc.dram_tensor("gdram", [E, TC], BF16)

    c0 = Cnt()
    with ExitStack() as _st:
        oc1f = _st.enter_context(nc.sbuf_tensor("oc1f_s", [1, 128], F32))
        ocb = _st.enter_context(nc.sbuf_tensor("ocb_s", [128, 1], BF16))
        oc64 = _st.enter_context(nc.sbuf_tensor("oc64_s", [1, 64], BF16))
        or512 = _st.enter_context(nc.sbuf_tensor("or512_s", [1, 512], BF16))
        i128 = _st.enter_context(nc.sbuf_tensor("i128_s", [128, 128], BF16))
        epsb = _st.enter_context(nc.sbuf_tensor("epsb", [1, 1], F32))
        bencc = _st.enter_context(nc.sbuf_tensor("bencc", [128, 16], F32))
        bs2cc = _st.enter_context(nc.sbuf_tensor("bs2cc", [64, 1], F32))
        bc2sc = _st.enter_context(nc.sbuf_tensor("bc2sc", [128, 16], F32))
        lgb = _st.enter_context(nc.sbuf_tensor("lgb", [128, 16], F32))
        bdrow = _st.enter_context(nc.sbuf_tensor("bdrow", [1, D], BF16))
        wrb = _st.enter_context(nc.sbuf_tensor("wrb", [64, E], BF16))
        brr = _st.enter_context(nc.sbuf_tensor("brr", [1, E], BF16))
        rtc = _st.enter_context(nc.sbuf_tensor("rtc", [128, 8], F32))
        grt = _st.enter_context(nc.sbuf_tensor("grt", [128, 16], F32))
        gbt = _st.enter_context(nc.sbuf_tensor("gbt", [128, 8], BF16))
        gtmp = _st.enter_context(nc.sbuf_tensor("gtmp", [8, 128], BF16))
        ws2cb = _st.enter_context(nc.sbuf_tensor("ws2cb", [128, 16 * 64], BF16))
        wc2sb = _st.enter_context(nc.sbuf_tensor("wc2sb", [64, H], BF16))
        wdecb = _st.enter_context(nc.sbuf_tensor("wdecb", [128, 16 * 1024], BF16))
        eo = _st.enter_context(nc.sbuf_tensor("eo", [64, TC], BF16))
        psA = _st.enter_context(nc.psum_tensor("psA", [128, 512], F32))
        psB = _st.enter_context(nc.psum_tensor("psB", [128, 512], F32))
        psE = _st.enter_context(nc.psum_tensor("psE", [128, 512], F32))
        psH = _st.enter_context(nc.psum_tensor("psH", [128, 512], F32))
        psC = _st.enter_context(nc.psum_tensor("psC", [128, 512], F32))
        psG = _st.enter_context(nc.psum_tensor("psG", [128, 1024], BF16))
        psSum = _st.enter_context(nc.psum_tensor("psSum", [128, 512], F32))
        psSq = _st.enter_context(nc.psum_tensor("psSq", [128, 512], F32))
        s_dh = _st.enter_context(nc.semaphore("s_dh"))
        s_ds = _st.enter_context(nc.semaphore("s_ds"))
        s_t = _st.enter_context(nc.semaphore("s_t"))
        s_s = _st.enter_context(nc.semaphore("s_s"))
        s_v = _st.enter_context(nc.semaphore("s_v"))
        s_st = _st.enter_context(nc.semaphore("s_st"))
        s_h0 = _st.enter_context(nc.semaphore("s_h0"))
        block = _st.enter_context(nc.Block())
        sems = {"dh": s_dh, "ds": s_ds, "t": s_t, "s": s_s, "v": s_v, "st": s_st,
                "h0": s_h0}
        pr = Prog(nc)

        _cm_contb = nc.sbuf_tensor("contb", [64, TC], BF16); contb = _cm_contb.__enter__()
        _cm_gr = nc.sbuf_tensor("gr", [1, E * TC], BF16); gr = _cm_gr.__enter__()
        _cm_xbb = nc.sbuf_tensor("xbb", [128, 8 * TC], BF16); xbb = _cm_xbb.__enter__()
        _cm_wen = nc.sbuf_tensor("wencb", [128, 8 * H], BF16); wencb = _cm_wen.__enter__()
        _cm_spk = nc.sbuf_tensor("spkb", [128, 2 * 512], BF16); spkb = _cm_spk.__enter__()

        def sync_loads(sync):
            for hc in range(16):
                sync.dma_start(out=bencc[:, hc:hc + 1],
                               in_=benc_t[hc * 128:(hc + 1) * 128].rearrange("(p o) -> p o", o=1)
                               ).then_inc(s_dh, 16)
                sync.dma_start(out=bc2sc[:, hc:hc + 1],
                               in_=bc2s_t[hc * 128:(hc + 1) * 128].rearrange("(p o) -> p o", o=1)
                               ).then_inc(s_dh, 16)
                c0.inc("dh", 32)
            sync.dma_start(out=bs2cc[:, :],
                           in_=bs2c_t.rearrange("(p o) -> p o", o=1)).then_inc(s_dh, 16)
            c0.inc("dh", 16)
            for cc in range(8):
                sync.dma_start(out=lgb[:, cc:cc + 1],
                               in_=lng_t[cc * 128:(cc + 1) * 128].rearrange("(p o) -> p o", o=1)
                               ).then_inc(s_dh, 16)
                sync.dma_start(out=lgb[:, 8 + cc:9 + cc],
                               in_=lnb_t[cc * 128:(cc + 1) * 128].rearrange("(p o) -> p o", o=1)
                               ).then_inc(s_dh, 16)
                c0.inc("dh", 32)
            for dst, srct in ((oc1f, oc1f_t), (ocb, ocb_t), (oc64, oc64_t),
                              (or512, or512_t), (i128, i128_t)):
                sync.dma_start(out=dst[:, :], in_=srct[:, :]).then_inc(s_dh, 16)
                c0.inc("dh", 16)

        def gp_loads(gpsimd):
            for cc in range(8):
                gpsimd.dma_start(out=xbb[:, cc * TC:(cc + 1) * TC],
                                 in_=xT_t[cc * 128:(cc + 1) * 128, :]).then_inc(s_ds, 16)
                gpsimd.dma_start(out=wencb[:, cc * H:(cc + 1) * H],
                                 in_=wenc_t[cc * 128:(cc + 1) * 128, :]).then_inc(s_ds, 16)
                c0.inc("ds", 32)
            for hc in range(16):
                gpsimd.dma_start(out=ws2cb[:, hc * 64:(hc + 1) * 64],
                                 in_=ws2c_t[hc * 128:(hc + 1) * 128, :]).then_inc(s_ds, 16)
                gpsimd.dma_start(out=wdecb[:, hc * 1024:(hc + 1) * 1024],
                                 in_=wdec_t[hc * 128:(hc + 1) * 128, :]).then_inc(s_ds, 16)
                c0.inc("ds", 32)
            gpsimd.dma_start(out=wc2sb[:, :], in_=wc2s_t[:, :]).then_inc(s_ds, 16)
            gpsimd.dma_start(out=wrb[:, :], in_=wr_t[:, :]).then_inc(s_ds, 16)
            gpsimd.dma_start(out=brr[:, :], in_=br_t.rearrange("(o e) -> o e", o=1)
                             ).then_inc(s_ds, 16)
            gpsimd.dma_start(out=bdrow[:, :], in_=bdec_t.rearrange("(o d) -> o d", o=1)
                             ).then_inc(s_ds, 16)
            c0.inc("ds", 64)
            for e in range(E):
                gpsimd.dma_start(out=w1b[:, e * EH:(e + 1) * EH],
                                 in_=w1e_t[e, :, :]).then_inc(s_ds, 16)
                gpsimd.dma_start(out=b1r[:, e * EH:(e + 1) * EH],
                                 in_=b1e_t[e, :].rearrange("(o h) -> o h", o=1)
                                 ).then_inc(s_ds, 16)
                gpsimd.dma_start(out=b2r[:, e * M:(e + 1) * M],
                                 in_=b2e_t[e, :].rearrange("(o m) -> o m", o=1)
                                 ).then_inc(s_ds, 16)
                c0.inc("ds", 48)
                for hc8 in range(8):
                    gpsimd.dma_start(out=w2b[:, (e * 8 + hc8) * M:(e * 8 + hc8 + 1) * M],
                                     in_=w2e_t[e, hc8 * 128:(hc8 + 1) * 128, :]
                                     ).then_inc(s_ds, 16)
                    c0.inc("ds", 16)

        _cm_w1b = nc.sbuf_tensor("w1b", [64, E * EH], BF16); w1b = _cm_w1b.__enter__()
        _cm_b1r = nc.sbuf_tensor("b1r", [1, E * EH], BF16); b1r = _cm_b1r.__enter__()
        _cm_w2b = nc.sbuf_tensor("w2b", [128, E * 8 * M], BF16); w2b = _cm_w2b.__enter__()
        _cm_b2r = nc.sbuf_tensor("b2r", [1, E * M], BF16); b2r = _cm_b2r.__enter__()
        _cm_cg = nc.sbuf_tensor("cg", [64, 512], BF16); cg = _cm_cg.__enter__()
        _cm_hsb = nc.sbuf_tensor("hsb", [128, 2 * 512], BF16); hsb = _cm_hsb.__enter__()

        block.sync(sync_loads)
        block.gpsimd(gp_loads)
        DH0 = c0.get("dh")
        DS0 = c0.get("ds")
        for eng in ("t", "s", "v"):
            pr.wait(eng, "dh", DH0)
            pr.wait(eng, "ds", DS0)
        pr.run("v", lambda v: v.memset(epsb[:, :], EPS))

        psCt = [psE, psH, psSum, psSq]
        sig_done = {}
        cont_mm_done = {}
        spk_gi = 0
        for hc in range(16):
            par = hc % 2
            for tci in range(4):
                bank = psA if spk_gi % 2 == 0 else psB
                if spk_gi >= 2:
                    pr.wait("t", "s", sig_done[spk_gi - 2])
                if hc >= 2 and tci == 0:
                    pass
                def mkspk(hc=hc, tci=tci, bank=bank):
                    def f(tensor):
                        mm = None
                        for ic in range(8):
                            mm = tensor.matmul(
                                bank[:, :],
                                wencb[:, ic * H + hc * 128: ic * H + (hc + 1) * 128],
                                xbb[:, ic * TC + tci * 512: ic * TC + (tci + 1) * 512],
                                start=(ic == 0), stop=(ic == 7))
                        return mm
                    return f
                td = pr.run("t", mkspk())
                pr.wait("s", "t", td)
                if hc >= 2:
                    pr.wait("s", "t", cont_mm_done.get((hc - 2, tci), 0))
                sd = pr.run("s", (lambda par=par, hc=hc, bank=bank: lambda s: s.activation(
                    spkb[:, par * 512:(par + 1) * 512], bank[:, :], AF.Sigmoid,
                    bias=bencc[:, hc:hc + 1], scale=1.0))())
                sig_done[spk_gi] = sd
                pr.wait("t", "s", sd)
                def mkcont(hc=hc, tci=tci, par=par):
                    def f(tensor):
                        return tensor.matmul(
                            psCt[tci][0:64, :], ws2cb[:, hc * 64:(hc + 1) * 64],
                            spkb[:, par * 512:(par + 1) * 512],
                            start=(hc == 0), stop=(hc == 15))
                    return f
                cont_mm_done[(hc, tci)] = pr.run("t", mkcont())
                spk_gi += 1
        for tci in range(4):
            pr.wait("s", "t", cont_mm_done[(15, tci)])
            pr.run("s", (lambda tci=tci: lambda s: s.activation(
                contb[:, tci * 512:(tci + 1) * 512], psCt[tci][0:64, :], AF.Identity,
                bias=bs2cc[:, :], scale=1.0))())
        S_CONT = pr.c.get("s")

        pr.wait("t", "s", S_CONT)
        for tt in range(16):
            def mklog(tt=tt):
                def f(tensor):
                    tensor.matmul(psC[0:128, 0:E], contb[:, tt * 128:(tt + 1) * 128],
                                  wrb[:, :], start=True, stop=False)
                    return tensor.matmul(psC[0:128, 0:E], or512[:, 0:128], brr[:, :],
                                         start=False, stop=True)
                return f
            td = pr.run("t", mklog())
            pr.wait("v", "t", td)
            pr.run("v", lambda v: v.tensor_reduce(out=rtc[:, 0:1], in_=psC[0:128, 0:E],
                                                  axis=mybir.AxisListType.X, op=AluOpType.max))
            pr.run("v", lambda v: v.tensor_scalar(out=rtc[:, 1:2], in0=rtc[:, 0:1],
                                                  scalar1=-1.0, scalar2=None,
                                                  op0=AluOpType.mult))
            V_NM = pr.c.get("v")
            pr.wait("s", "v", V_NM)
            sd = pr.run("s", lambda s: s.activation(grt[:, 0:8], psC[0:128, 0:E], AF.Exp,
                                                    bias=rtc[:, 1:2], scale=1.0))
            pr.wait("v", "s", sd)
            pr.run("v", lambda v: v.tensor_reduce(out=rtc[:, 2:3], in_=grt[:, 0:8],
                                                  axis=mybir.AxisListType.X, op=AluOpType.add))
            pr.run("v", lambda v: v.reciprocal(out=rtc[:, 3:4], in_=rtc[:, 2:3]))
            pr.run("v", lambda v: v.tensor_scalar(out=grt[:, 0:8], in0=grt[:, 0:8],
                                                  scalar1=rtc[:, 3:4], scalar2=None,
                                                  op0=AluOpType.mult))
            pr.run("v", lambda v: v.tensor_reduce(out=rtc[:, 4:5], in_=grt[:, 0:8],
                                                  axis=mybir.AxisListType.X, op=AluOpType.max))
            pr.run("v", lambda v: v.tensor_scalar(out=grt[:, 8:16], in0=grt[:, 0:8],
                                                  scalar1=rtc[:, 4:5], scalar2=None,
                                                  op0=AluOpType.is_ge))
            pr.run("v", lambda v: v.scalar_tensor_tensor(out=grt[:, 8:16], in0=grt[:, 8:16],
                                                         scalar=-1e9, in1=grt[:, 0:8],
                                                         op0=AluOpType.mult, op1=AluOpType.add))
            pr.run("v", lambda v: v.tensor_reduce(out=rtc[:, 5:6], in_=grt[:, 8:16],
                                                  axis=mybir.AxisListType.X, op=AluOpType.max))
            pr.run("v", lambda v: v.tensor_tensor(out=rtc[:, 6:7], in0=rtc[:, 4:5],
                                                  in1=rtc[:, 5:6], op=AluOpType.add))
            pr.run("v", lambda v: v.reciprocal(out=rtc[:, 7:8], in_=rtc[:, 6:7]))
            pr.run("v", lambda v: v.tensor_scalar(out=grt[:, 8:16], in0=grt[:, 0:8],
                                                  scalar1=rtc[:, 5:6], scalar2=None,
                                                  op0=AluOpType.is_ge))
            pr.run("v", lambda v: v.tensor_scalar(out=grt[:, 0:8], in0=grt[:, 0:8],
                                                  scalar1=rtc[:, 7:8], scalar2=None,
                                                  op0=AluOpType.mult))
            vg = pr.run("v", lambda v: v.tensor_tensor(out=gbt[:, 0:8], in0=grt[:, 0:8],
                                                       in1=grt[:, 8:16], op=AluOpType.mult))
            pr.wait("t", "v", vg)
            tg = pr.run("t", lambda t: t.transpose(psG[0:8, 0:128], gbt[:, 0:8], i128[:, :]))
            pr.wait("v", "t", tg)
            if tt >= 1:
                pr.wait("v", "h0", 16 * tt)
            pr.run("v", lambda v: v.tensor_copy(out=gtmp[:, :], in_=psG[0:8, 0:128]))
            c0.v.setdefault("gstore_marks", []).append(pr.c.get("v"))
        V_ROUTER = pr.c.get("v")

        def sync_gstore(sync):
            for tt in range(16):
                sync.wait_ge(s_v, c0.v["gstore_marks"][tt])
                sync.dma_start(out=gdram.ap()[:, tt * 128:(tt + 1) * 128],
                               in_=gtmp[:, :]).then_inc(s_h0, 16)
            sync.wait_ge(s_h0, 256)
            sync.dma_start(out=gr[:, :],
                           in_=gdram.ap().rearrange("e t -> (e t)").rearrange("(o n) -> o n", o=1)
                           ).then_inc(s_h0, 16)
        block.sync(sync_gstore)

        pr.wait("t", "h0", 272)
        eo_done = {}
        relu_done = {}
        cg_done = {}
        for tci in range(4):
            first = True
            for e in range(E):
                def mkgb(e=e, tci=tci):
                    def f(tensor):
                        return tensor.matmul(psC[0:64, :], oc64[:, :],
                                             gr[:, e * TC + tci * 512: e * TC + (tci + 1) * 512],
                                             start=True, stop=True)
                    return f
                if (tci, e) != (0, 0):
                    pr.wait("t", "v", cg_done.get((tci, e - 1) if e > 0 else (tci - 1, E - 1), 0))
                tb = pr.run("t", mkgb())
                pr.wait("v", "t", tb)
                vd = pr.run("v", (lambda tci=tci: lambda v: v.tensor_tensor(
                    out=cg[:, :], in0=psC[0:64, :], in1=contb[:, tci * 512:(tci + 1) * 512],
                    op=AluOpType.mult))())
                cg_done[(tci, e)] = vd
                pr.wait("t", "v", vd)
                for eh in range(8):
                    par = eh % 2
                    ei = e * 8 + eh
                    if eh >= 2 or e > 0 or tci > 0:
                        key = relu_done.get((tci, e, eh - 2)) or \
                              relu_done.get((tci, e - 1, 6 + eh)) or \
                              relu_done.get((tci - 1, E - 1, 6 + eh))
                    def mkh(e=e, eh=eh, tci=tci):
                        def f(tensor):
                            tensor.matmul(psH[:, :],
                                          w1b[:, e * EH + eh * 128: e * EH + (eh + 1) * 128],
                                          cg[:, :], start=True, stop=False)
                            return tensor.matmul(
                                psH[:, :], b1r[:, e * EH + eh * 128: e * EH + (eh + 1) * 128],
                                gr[:, e * TC + tci * 512: e * TC + (tci + 1) * 512],
                                start=False, stop=True)
                        return f
                    th = pr.run("t", mkh())
                    pr.wait("s", "t", th)
                    prev = (e * 8 + eh - 2)
                    if tci > 0 or e > 0 or eh >= 2:
                        pr.wait("s", "t", pr.c.get("w2mm_%d" % par, 0))
                    sd = pr.run("s", (lambda par=par: lambda s: s.activation(
                        hsb[:, par * 512:(par + 1) * 512], psH[:, :], AF.Relu,
                        scale=1.0))())
                    relu_done[(tci, e, eh)] = sd
                    pr.wait("t", "s", sd)
                    def mkw2(e=e, eh=eh, tci=tci, par=par,
                             start=(e == 0 and eh == 0), stop=False):
                        def f(tensor):
                            return tensor.matmul(
                                psE[0:64, :], w2b[:, (e * 8 + eh) * M:(e * 8 + eh + 1) * M],
                                hsb[:, par * 512:(par + 1) * 512],
                                start=start, stop=stop)
                        return f
                    tw = pr.run("t", mkw2())
                    pr.c.v["w2mm_%d" % par] = tw
                def mkb2(e=e, tci=tci, stop=(e == E - 1)):
                    def f(tensor):
                        return tensor.matmul(psE[0:64, :], b2r[:, e * M:(e + 1) * M],
                                             gr[:, e * TC + tci * 512: e * TC + (tci + 1) * 512],
                                             start=False, stop=stop)
                    return f
                pr.run("t", mkb2())
            TEOD = pr.c.get("t")
            pr.wait("v", "t", TEOD)
            ed = pr.run("v", (lambda tci=tci: lambda v: v.tensor_copy(
                out=eo[:, tci * 512:(tci + 1) * 512], in_=psE[0:64, :]))())
            eo_done[tci] = ed
        V_EO = pr.c.get("v")

        _cm_hsb.__exit__(None, None, None)
        _cm_cg.__exit__(None, None, None)
        _cm_b2r.__exit__(None, None, None)
        _cm_w2b.__exit__(None, None, None)
        _cm_b1r.__exit__(None, None, None)
        _cm_w1b.__exit__(None, None, None)
        _cm_spk.__exit__(None, None, None)
        _cm_wen.__exit__(None, None, None)
        _cm_xbb.__exit__(None, None, None)
        _cm_gr.__exit__(None, None, None)
        _cm_contb.__exit__(None, None, None)
        _cm_dec = nc.sbuf_tensor("dec", [128, 8 * TC], BF16); dec = _cm_dec.__enter__()
        _cm_xb = nc.sbuf_tensor("xb", [128, 8 * TC], F32); xb = _cm_xb.__enter__()
        _cm_smtc = nc.sbuf_tensor("smtc", [128, 16 * 512], BF16); smtc = _cm_smtc.__enter__()

        def sync_loads_x(sync):
            sync.wait_ge(s_v, V_EO)
            for cc in range(8):
                sync.dma_start(out=xb[:, cc * TC:(cc + 1) * TC],
                               in_=xT_t[cc * 128:(cc + 1) * 128, :]).then_inc(s_dh, 16)
                c0.inc("dh", 16)
        block.sync(sync_loads_x)
        DH1 = c0.get("dh")

        evac_marks = []
        dec_gi = 0
        for tci in range(4):
            sm_marks = {}
            for hc in range(16):
                if hc >= 1:
                    pr.wait("t", "s", sm_marks[hc - 1])
                def mksm(hc=hc, tci=tci):
                    def f(tensor):
                        return tensor.matmul(psC[0:128, :], wc2sb[:, hc * 128:(hc + 1) * 128],
                                             eo[:, tci * 512:(tci + 1) * 512],
                                             start=True, stop=True)
                    return f
                tsm = pr.run("t", mksm())
                pr.wait("s", "t", tsm)
                if tci > 0:
                    pr.wait("s", "t", pr.c.get("dec_done_%d" % (tci - 1), 0))
                sd = pr.run("s", (lambda hc=hc: lambda s: s.activation(
                    smtc[:, hc * 512:(hc + 1) * 512], psC[0:128, :], AF.Sigmoid,
                    bias=bc2sc[:, hc:hc + 1], scale=1.0))())
                sm_marks[hc] = sd
            pr.wait("t", "s", sm_marks[15])
            for oc in range(8):
                bank = psA if dec_gi % 2 == 0 else psB
                if dec_gi >= 2:
                    pr.wait("t", "v", evac_marks[dec_gi - 2])
                def mkdec(oc=oc, bank=bank):
                    def f(tensor):
                        for hc in range(16):
                            tensor.matmul(bank[:, :],
                                          wdecb[:, hc * 1024 + oc * 128: hc * 1024 + (oc + 1) * 128],
                                          smtc[:, hc * 512:(hc + 1) * 512],
                                          start=(hc == 0), stop=False)
                        return tensor.matmul(bank[:, :], bdrow[:, oc * 128:(oc + 1) * 128],
                                             or512[:, :], start=False, stop=True)
                    return f
                td = pr.run("t", mkdec())
                pr.wait("v", "t", td)
                vd = pr.run("v", (lambda oc=oc, tci=tci, bank=bank: lambda v: v.tensor_copy(
                    out=dec[:, oc * TC + tci * 512: oc * TC + (tci + 1) * 512],
                    in_=bank[:, :]))())
                evac_marks.append(vd)
                dec_gi += 1
            pr.c.v["dec_done_%d" % tci] = pr.c.get("t")

        _cm_smtc.__exit__(None, None, None)
        _cm_scr = nc.sbuf_tensor("scr", [128, 4 * 512], BF16); scr = _cm_scr.__enter__()
        _cm_r4 = nc.sbuf_tensor("rows4", [1, 4 * TC], F32); rows4 = _cm_r4.__enter__()
        _cm_bcm = nc.sbuf_tensor("bcm", [128, TC], F32); bcm = _cm_bcm.__enter__()
        _cm_bcs = nc.sbuf_tensor("bcs", [128, TC], F32); bcs = _cm_bcs.__enter__()
        _cm_xo2 = nc.sbuf_tensor("xo2", [128, TC], F32); xo2 = _cm_xo2.__enter__()
        rows = [rows4[:, k * TC:(k + 1) * TC] for k in range(4)]

        decbuf = [dec[:, oc * TC:(oc + 1) * TC] for oc in range(8)]
        pr.wait("v", "dh", DH1)
        emit_ln_tail(pr, decbuf, scr, rows, bcm, bcs,
                     lambda ch: xo2[:, :],
                     psSum, psSq, psC, ocb, oc1f, epsb,
                     gcol=lgb[:, 0:8], bcol=lgb[:, 8:16],
                     pre_wait=("t", pr.c.get("t")),
                     post_add_ap=lambda ch: xb[:, ch * TC:(ch + 1) * TC])
        rbuf2 = [xb[:, oc * TC:(oc + 1) * TC] for oc in range(8)]

        def chunk_prewait(ch):
            if ch >= 1:
                return ("st", 16 * ch)
            return None
        ready = emit_ln_tail(pr, rbuf2, scr, rows, bcm, bcs,
                             lambda ch: xo2[:, :],
                             psSum, psSq, psC, ocb, oc1f, epsb,
                             chunk_prewait=chunk_prewait)
        pr.emit(block, sems)

        def sync_stores(sync):
            for ch in range(8):
                sync.wait_ge(s_v, ready[ch])
                sync.dma_start(out=xo_t[ch * 128:(ch + 1) * 128, :],
                               in_=xo2[:, :]).then_inc(s_st, 16)
            sync.wait_ge(s_st, 128)
        block.sync(sync_stores)
        _cm_xo2.__exit__(None, None, None)
        _cm_bcs.__exit__(None, None, None)
        _cm_bcm.__exit__(None, None, None)
        _cm_r4.__exit__(None, None, None)
        _cm_scr.__exit__(None, None, None)
        _cm_xb.__exit__(None, None, None)
        _cm_dec.__exit__(None, None, None)
    return nc



_PROG_CACHE = {}


def _get_prog(name, builder):
    if name not in _PROG_CACHE:
        _PROG_CACHE[name] = builder()
    return _PROG_CACHE[name]


def run_l0_qkv(x, Wq, bq_, Wk, bk_, Wv, bv_):
    nc = _get_prog("l0_qkv", build_l0_qkv)
    ropeR = host_rope_R()
    in_maps = []
    for core in range(NCORE):
        b, half = divmod(core, 2)
        xs = np.ascontiguousarray(x[b, half * TC:(half + 1) * TC, :].T)
        in_maps.append({
            "xT": xs, "wq": Wq, "wk": Wk, "wv": Wv,
            "bq": bq_, "bk": bk_, "bv": bv_, "ropeR": ropeR,
        })
    res = run_bass_kernel_spmd(nc, in_maps, core_ids=list(range(NCORE)))
    q = np.zeros((B, D, T), NP_BF16)
    k = np.zeros((B, HD, T), NP_BF16)
    v = np.zeros((B, T, HD), NP_BF16)
    for core in range(NCORE):
        b, half = divmod(core, 2)
        sl = slice(half * TC, (half + 1) * TC)
        q[b, :, sl] = res.results[core]["q"]
        k[b, :, sl] = res.results[core]["k"]
        v[b, sl, :] = res.results[core]["v"]
    return q, k, v


def run_l1_ffn(x3, z):
    nc = _get_prog("l1_ffn", build_l1_ffn)
    consts = {
        "oc1f": np.ones((1, 128), np.float32),
        "ocb": np.ones((128, 1), dtype=NP_BF16),
        "oc64": np.ones((1, 64), dtype=NP_BF16),
        "or512": np.ones((1, 512), dtype=NP_BF16),
        "i128": np.eye(128, dtype=NP_BF16),
    }
    in_maps = []
    for core in range(NCORE):
        b, half = divmod(core, 2)
        m = dict(consts)
        m.update({
            "xT": np.ascontiguousarray(x3[b, half * TC:(half + 1) * TC, :].T),
            "wenc": z["L1_ffn_W_enc"], "benc": z["L1_ffn_b_enc"],
            "ws2c": z["L1_ffn_W_s2c"], "bs2c": z["L1_ffn_b_s2c"],
            "wr": z["L1_ffn_Wr"], "br": z["L1_ffn_br"],
            "w1e": z["L1_ffn_W1"], "b1e": z["L1_ffn_b1"],
            "w2e": z["L1_ffn_W2"], "b2e": z["L1_ffn_b2"],
            "wc2s": z["L1_ffn_W_c2s"], "bc2s": z["L1_ffn_b_c2s"],
            "wdec": z["L1_ffn_W_dec"], "bdec": z["L1_ffn_b_dec"],
            "lng": z["L1_ffn_ln_g"], "lnb": z["L1_ffn_ln_b"],
        })
        in_maps.append(m)
    res = run_bass_kernel_spmd(nc, in_maps, core_ids=list(range(NCORE)))
    out = np.zeros((B, T, D), np.float32)
    for core in range(NCORE):
        b, half = divmod(core, 2)
        out[b, half * TC:(half + 1) * TC, :] = res.results[core]["xo"].T
    return out


def kernel(**inputs):
    z = inputs
    x = np.ascontiguousarray(z["L0_attn_x"], dtype=np.float32)
    q, k, v = run_l0_qkv(x, z["L0_attn_Wq"], z["L0_attn_bq"], z["L0_attn_Wk"],
                         z["L0_attn_bk"], z["L0_attn_Wv"], z["L0_attn_bv"])
    attn = run_attn(q, k, v, mqa=True)
    x1 = run_proj_ln(attn, x, z["L0_attn_Wo"], z["L0_attn_bo"])
    x2 = run_l0_ffn(x1, z["L0_ffn_w1"], z["L0_ffn_conv_w"], z["L0_ffn_conv_b"],
                    z["L0_ffn_w2"])
    q, k, v = run_l1_qkv(x2, z["L1_attn_w_qkv"])
    attn = run_attn(q, k, v, mqa=False)
    x3 = run_proj_ln(attn, x2, z["L1_attn_w_proj"], None)
    return run_l1_ffn(x3, z)
